# version 23
# baseline (speedup 1.0000x reference)
import math
import numpy as np
import ml_dtypes
from contextlib import ExitStack

import concourse.bass as bass
import concourse.mybir as mybir
from concourse.bass_utils import run_bass_kernel_spmd

F32 = mybir.dt.float32
BF16 = mybir.dt.bfloat16
I32 = mybir.dt.int32
AF = mybir.ActivationFunctionType
ALU = mybir.AluOpType
AX = mybir.AxisListType

NEG = -1.0e30
SCLAMP = 1280.0


class Cfg:
    def __init__(self, S=4096, E=1024, NH=16, HID=3072, NB=4):
        self.S, self.E, self.NH, self.HID, self.NB = S, E, NH, HID, NB
        self.DH = 64
        self.NP = NH // 2
        self.EC = E // 128
        self.KVC = S // 128
        self.QBS = S // 8
        self.DGC = self.QBS // 128
        self.OWN = S // 2
        self.OWNC = self.OWN // 128
        self.NHC = HID // 128
        self.EXT = [(j + 1) * 2 * self.DGC for j in range(4)]
        self.QW = min(512, self.OWN)
        self.NQW = self.OWN // self.QW
        self.KW = min(512, S, 2 * self.QBS)
        self.NKG = S // self.KW
        self.EW = min(512, E)
        self.NEW = E // self.EW
        self.GT = min(4, self.EC)
        self.SM = 1.0 / math.sqrt(E)
        self.ncores = 2 * NB


FULL = Cfg()



def core_layout(cfg, t):
    D = cfg.DGC
    own = np.concatenate([np.arange(cfg.QBS) + cfg.QBS * (2 * j + t)
                          for j in range(4)])
    perm = np.arange(cfg.KVC)
    if t == 0:
        perm = perm.reshape(-1, 2, D)[:, ::-1, :].reshape(-1)
    kvrows = (perm[:, None] * 128 + np.arange(128)[None, :]).reshape(-1)
    return own, perm, kvrows


def wl_table(cfg, t):
    D = cfg.DGC
    wl = np.zeros((4, cfg.KVC), np.float32)
    for j in range(4):
        ext = cfg.EXT[j]
        for c in range(cfg.KVC):
            if c >= ext:
                wl[j, c] = NEG
            elif c >= ext - D:
                wl[j, c] = 0.0
            elif t == 0 and 2 * D * j <= c < 2 * D * j + D:
                wl[j, c] = NEG
    return wl


def pos_table(cfg):
    EPS = float(np.finfo(np.float32).eps)
    idx = np.arange(cfg.S, dtype=np.float64)
    scale = 1.0 / (10000.0 ** (np.arange(0, 64, 2, dtype=np.float64) / 64.0) + EPS)
    ang = idx[:, None] * scale[None, :]
    pos = np.stack([np.sin(ang), np.cos(ang)], axis=-1).reshape(cfg.S, 64)
    return pos.astype(np.float32)


def host_inputs(cfg, x, Wq, Wk, Wv, Wo, wg, wu, wd):
    bf = ml_dtypes.bfloat16
    pos = pos_table(cfg)
    ident = np.eye(128, dtype=bf)
    tri = np.zeros((128, cfg.DGC * cfg.QBS), np.float32)
    for r in range(cfg.DGC):
        k = 128 * r + np.arange(128)[:, None]
        q = np.arange(cfg.QBS)[None, :]
        tri[:, r * cfg.QBS:(r + 1) * cfg.QBS] = (k <= q)
    tri = tri.astype(bf)
    onescol = np.ones((128, 1), dtype=bf)
    onesrow = np.ones((1, 128), dtype=bf)
    cvec = np.zeros((128, 4), np.float32)
    cvec[:, 0] = 1e-5

    shared = dict(
        ident=ident, tri=tri, onescol=onescol, onesrow=onesrow, cvec=cvec,
        Wq=np.ascontiguousarray(Wq), Wk=np.ascontiguousarray(Wk),
        Wv=np.ascontiguousarray(Wv), Wo=np.ascontiguousarray(Wo),
        wg=np.ascontiguousarray(wg), wu=np.ascontiguousarray(wu),
        wd=np.ascontiguousarray(wd),
    )
    maps, owns = [], []
    for c in range(cfg.ncores):
        b, t = c // 2, c % 2
        own, perm, kvrows = core_layout(cfg, t)
        wl = wl_table(cfg, t)
        WL = np.repeat(wl.reshape(1, -1), 128, axis=0).astype(np.float32)
        pq = pos[own].T
        pk = pos[kvrows].T
        m = dict(shared)
        m["x_q"] = np.ascontiguousarray(x[b][own])
        m["x_kv"] = np.ascontiguousarray(x[b][kvrows])
        m["posq"] = np.ascontiguousarray(np.concatenate([pq, pq], 0).astype(np.float32))
        m["posk"] = np.ascontiguousarray(np.concatenate([pk, pk], 0).astype(np.float32))
        m["WL"] = WL
        maps.append(m)
        owns.append((b, own))
    return maps, owns



class PB:
    ENG = ("SY", "G", "T", "V", "S")

    def __init__(self, nc):
        self.nc = nc
        self.q = {k: [] for k in self.ENG}
        self.sems = {}
        self.hw = {k: {} for k in self.ENG}

    def mksem(self, stack, *names):
        for n in names:
            self.sems[n] = [stack.enter_context(self.nc.semaphore(n)), 0]

    def cnt(self, name):
        return (name, self.sems[name][1])

    def op(self, qn, fn, waits=(), inc=None, amt=1):
        w = [x for x in waits if x is not None]
        self.q[qn].append((fn, w, inc, amt))
        if inc is not None:
            self.sems[inc][1] += amt
            return (inc, self.sems[inc][1])
        return None

    def flush(self, name):
        def runner(qn, ops):
            def body(eng):
                hw = self.hw[qn]
                for fn, waits, inc, amt in ops:
                    for (sn, val) in waits:
                        if val > hw.get(sn, 0):
                            eng.wait_ge(self.sems[sn][0], val)
                            hw[sn] = val
                    ins = fn(eng)
                    if inc is not None:
                        ins.then_inc(self.sems[inc][0], amt)
            return body

        with self.nc.Block(name) as block:
            tgt = dict(SY=block.sync, G=block.gpsimd, T=block.tensor,
                       V=block.vector, S=block.scalar)
            for qn in self.ENG:
                if self.q[qn]:
                    tgt[qn](runner(qn, self.q[qn]))
                    self.q[qn] = []


def build_program(cfg):
    nc = bass.Bass("TRN2", target_bir_lowering=False)
    S, E, NP, EC, KVC = cfg.S, cfg.E, cfg.NP, cfg.EC, cfg.KVC
    QBS, DGC, OWN, OWNC = cfg.QBS, cfg.DGC, cfg.OWN, cfg.OWNC
    NHC, QW, NQW, KW, NKG = cfg.NHC, cfg.QW, cfg.NQW, cfg.KW, cfg.NKG
    GT, EW, NEW = cfg.GT, cfg.EW, cfg.NEW
    C2 = 1.0 / (127.0 * 127.0)

    d = {}
    d["x_q"] = nc.dram_tensor("x_q", [OWN, E], F32, kind="ExternalInput").ap()
    d["x_kv"] = nc.dram_tensor("x_kv", [S, E], F32, kind="ExternalInput").ap()
    d["posq"] = nc.dram_tensor("posq", [128, OWN], F32, kind="ExternalInput").ap()
    d["posk"] = nc.dram_tensor("posk", [128, S], F32, kind="ExternalInput").ap()
    d["WL"] = nc.dram_tensor("WL", [128, 4 * KVC], F32, kind="ExternalInput").ap()
    d["tri"] = nc.dram_tensor("tri", [128, DGC * QBS], BF16, kind="ExternalInput").ap()
    d["ident"] = nc.dram_tensor("ident", [128, 128], BF16, kind="ExternalInput").ap()
    d["onescol"] = nc.dram_tensor("onescol", [128, 1], BF16, kind="ExternalInput").ap()
    d["onesrow"] = nc.dram_tensor("onesrow", [1, 128], BF16, kind="ExternalInput").ap()
    d["cvec"] = nc.dram_tensor("cvec", [128, 4], F32, kind="ExternalInput").ap()
    for w in ("Wq", "Wk", "Wv", "Wo"):
        d[w] = nc.dram_tensor(w, [E, E], F32, kind="ExternalInput").ap()
    d["wg"] = nc.dram_tensor("wg", [cfg.HID, E], I32, kind="ExternalInput").ap()
    d["wu"] = nc.dram_tensor("wu", [cfg.HID, E], I32, kind="ExternalInput").ap()
    d["wd"] = nc.dram_tensor("wd", [E, cfg.HID], I32, kind="ExternalInput").ap()
    d["out"] = nc.dram_tensor("out", [OWN, E], F32, kind="ExternalOutput").ap()
    gsp = nc.dram_tensor("g_spill", [OWN, E], F32, kind="Internal").ap()

    xq_t = d["x_q"].rearrange("(n p) e -> n p e", p=128)
    xkv_t = d["x_kv"].rearrange("(n p) e -> n p e", p=128)
    out_t = d["out"].rearrange("(n p) e -> n p e", p=128)
    gsp_t = gsp.rearrange("(n p) e -> n p e", p=128)
    wgr = d["wg"].rearrange("(h p) e -> h p e", p=128)
    wur = d["wu"].rearrange("(h p) e -> h p e", p=128)
    wdr = d["wd"].rearrange("(dc p) h -> p dc h", p=128)
    wqr = d["Wq"].rearrange("(ec p) e -> p ec e", p=128)
    wkr = d["Wk"].rearrange("(ec p) e -> p ec e", p=128)
    wvr = d["Wv"].rearrange("(ec p) e -> p ec e", p=128)
    wor = d["Wo"].rearrange("(rc p) e -> p rc e", p=128)

    P = PB(nc)

    with ExitStack() as top:
        P.mksem(top, "sT", "sS", "sV", "dC", "dX0", "dX1", "dW0", "dW1",
                "dS0", "dS1", "dV0", "dV1", "dT0", "dT1", "dTW0", "dTW1", "dG0", "dG1", "dR0", "dR1", "dR2", "dR3", "dO0", "dO1", "dP", "dK")
        e = top.enter_context

        sWL = e(nc.sbuf_tensor("sWL", [128, 4 * KVC], F32))
        sBM = [e(nc.sbuf_tensor(f"sBM{i}", [128, 4 * KVC], F32)) for i in range(2)]
        sID = e(nc.sbuf_tensor("sID", [128, 128], BF16))
        sONE = e(nc.sbuf_tensor("sONE", [128, 1], BF16))
        sOR = e(nc.sbuf_tensor("sOR", [1, 128], BF16))
        sCV = e(nc.sbuf_tensor("sCV", [128, 4], F32))
        RA = e(nc.sbuf_tensor("RA", [128, max(EC, NP) * OWN], BF16))
        big0c = max(NHC * OWN, EC * S + NP * OWN, EC * E)
        BIG0 = e(nc.sbuf_tensor("BIG0", [128, big0c], BF16))
        b5set = 2 * S + 2 * KVC * 130 + 4 * EC * 128 + 2 * 2 * QBS + 1024 + QBS
        big1c = max(NHC * E, b5set)
        BIG1 = e(nc.sbuf_tensor("BIG1", [128, big1c], BF16))
        sLN = [e(nc.sbuf_tensor(f"sLN{i}", [128, 8], F32)) for i in range(2)]
        sMQ = e(nc.sbuf_tensor("sMQ", [1, NP * 4], F32))
        sMK = e(nc.sbuf_tensor("sMK", [1, 8], F32))
        sBSB = e(nc.sbuf_tensor("sBSB", [1, 4], BF16))


        xn_own = RA[:, 0:EC * OWN].rearrange("p (ec t) -> p ec t", ec=EC)
        AO = RA[:, 0:NP * OWN].rearrange("p (np t) -> p np t", np=NP)
        ACT = BIG0[:, 0:NHC * OWN].rearrange("p (hc t) -> p hc t", hc=NHC)
        xn_kv = BIG0[:, 0:EC * S].rearrange("p (ec t) -> p ec t", ec=EC)
        Q_all = BIG0[:, EC * S:EC * S + NP * OWN].rearrange("p (np t) -> p np t", np=NP)
        Wo_sb = BIG0[:, 0:EC * E].rearrange("p (rc e) -> p rc e", rc=EC)
        wdT = BIG1[:, 0:NHC * E].rearrange("p (hc e) -> p hc e", hc=NHC)
        o = 0
        Kp = [BIG1[:, o + i * S: o + (i + 1) * S] for i in range(2)]
        o += 2 * S
        Vaug = [BIG1[:, o + i * KVC * 130: o + (i + 1) * KVC * 130]
                .rearrange("p (c two sv) -> p c two sv", two=2, sv=65) for i in range(2)]
        o += 2 * KVC * 130
        Wsl = [BIG1[:, o + i * EC * 128: o + (i + 1) * EC * 128]
               .rearrange("p (ec dd) -> p ec dd", ec=EC) for i in range(4)]
        o += 4 * EC * 128
        expb = [BIG1[:, o + i * 2 * QBS: o + (i + 1) * 2 * QBS] for i in range(2)]
        o += 2 * 2 * QBS
        sSQ = [BIG1[:, o:o + 512], BIG1[:, o + 512:o + 1024]]
        o += 1024
        sRS = BIG1[:, o:o + QBS]

        def ld(dst, src):
            return lambda sy, dst=dst, src=src: sy.dma_start(out=dst, in_=src)

        nconst = 0
        for dst, src in ((sWL[:], d["WL"]), (sID[:], d["ident"]),
                         (sONE[:], d["onescol"]),
                         (sOR[:], d["onesrow"]), (sCV[:], d["cvec"])):
            P.op("SY", ld(dst, src), inc="dC", amt=16)
            nconst += 1
        tok_consts = ("dC", 16 * nconst)

        def ln_phase(xt_src, ntile, xn_dst, dsem, gates, tag, pool_ext=None,
                     xnt_ext=None, xnt_gate=None,
                     dt_gate=None, sq_gate=None, interleave_cb=None,
                     sq_ext=None):
            with ExitStack() as ph:
                pe = ph.enter_context
                if pool_ext is not None:
                    xst, xnt = pool_ext
                else:
                    xst = [pe(nc.sbuf_tensor(f"xst{tag}{i}", [128, E], F32))
                           for i in range(2)]
                    xnt = xnt_ext if xnt_ext is not None else [
                        pe(nc.sbuf_tensor(f"xnt{tag}{i}", [128, E], BF16))
                        for i in range(2)]
                sq = sq_ext if sq_ext is not None else [
                    pe(nc.psum_tensor(f"sqb{tag}{i}", [128, E], F32))
                    for i in range(2)]
                xst_free = [None, None]
                sq_free = list(sq_gate) if sq_gate else [None, None]
                xnt_free = [xnt_gate, xnt_gate]
                chain = [None, None]
                dt_cur = [None, None]
                last_ev = None
                tile_toks = []
                for n in range(ntile):
                    par = n % 2
                    w = [xst_free[par]] + (gates if n < 2 else [])
                    tdma = P.op("SY", ld(xst[par][:], xt_src[n]),
                                waits=w, inc=dsem[par], amt=16)
                    L = sLN[par]
                    t1 = P.op("V", lambda v, L=L, b=xst[par][:]:
                              v.reduce_sum(L[:, 0:1], b, axis=AX.X),
                              waits=[tdma, chain[par]], inc="sV")
                    t2 = P.op("S", lambda s, a=sq[par][:], b=xst[par][:]:
                              s.activation(a, b, AF.Square),
                              waits=[tdma, sq_free[par]], inc="sS")
                    t3 = P.op("V", lambda v, L=L, a=sq[par][:]:
                              v.reduce_sum(L[:, 1:2], a, axis=AX.X),
                              waits=[t2], inc="sV")
                    sq_free[par] = t3
                    t4 = P.op("S", lambda s, L=L: s.activation(
                        L[:, 2:3], L[:, 0:1], AF.Copy, scale=1.0 / E),
                        waits=[t1], inc="sS")
                    t5 = P.op("S", lambda s, L=L: s.activation(
                        L[:, 3:4], L[:, 2:3], AF.Square),
                        waits=[t4], inc="sS")
                    t5b = P.op("S", lambda s, L=L: s.activation(
                        L[:, 5:6], L[:, 3:4], AF.Copy, scale=-1.0, bias=1e-5),
                        waits=[t5], inc="sS")
                    t6 = P.op("S", lambda s, L=L: s.activation(
                        L[:, 4:5], L[:, 1:2], AF.Copy, scale=1.0 / E),
                        waits=[t3], inc="sS")
                    t8 = P.op("S", lambda s, L=L: s.activation(
                        L[:, 6:7], L[:, 4:5], AF.Ln, bias=L[:, 5:6]),
                        waits=[t5b, t6], inc="sS")
                    t9 = P.op("S", lambda s, L=L: s.activation(
                        L[:, 7:8], L[:, 6:7], AF.Exp, scale=-0.5),
                        waits=[t8], inc="sS")
                    ta = P.op("S", lambda s, L=L: s.activation(
                        L[:, 3:4], L[:, 2:3], AF.Copy, scale=-1.0),
                        waits=[t9], inc="sS")
                    tb = P.op("S", lambda s, L=L: s.activation(
                        L[:, 5:6], L[:, 3:4], AF.Copy, scale=L[:, 7:8]),
                        waits=[ta], inc="sS")
                    t11 = P.op("V", lambda v, L=L, a=xnt[par][:], b=xst[par][:]:
                               v.tensor_scalar(a, b, L[:, 7:8], L[:, 5:6],
                                               ALU.mult, ALU.add),
                               waits=[tb, xnt_free[par]], inc="sV")
                    chain[par] = t11
                    xst_free[par] = t11
                    qn, dsT = ("S", "dT0") if par == 0 else ("SY", "dT1")
                    tdt = P.op(qn, lambda e,
                               o=xn_dst[:, :, n * 128:(n + 1) * 128],
                               i_=xnt[par][:]:
                               e.dma_start_transpose(out=o, in_=i_),
                               waits=[t11, dt_gate], inc=dsT, amt=16)
                    xnt_free[par] = tdt
                    dt_cur[par] = tdt
                    last_ev = [t for t in dt_cur if t is not None]
                    tile_toks.append(last_ev)
                    if interleave_cb is not None:
                        interleave_cb(n, tile_toks)
                return last_ev, tile_toks

        with ExitStack() as phb0:
            pb0 = phb0.enter_context
            xstA = [pb0(nc.sbuf_tensor(f"xstA{i}", [128, E], F32))
                    for i in range(2)]
            ph = ExitStack()
            pe = ph.enter_context
            wgc = [pe(nc.sbuf_tensor(f"wgc{i}", [128, E], BF16)) for i in range(2)]
            wuc = [pe(nc.sbuf_tensor(f"wuc{i}", [128, E], BF16)) for i in range(2)]
            wdc = [pe(nc.sbuf_tensor("wdc0", [128, E], BF16))]
            wgT = [pe(nc.sbuf_tensor(f"wgT{i}", [128, EC * 128], BF16))
                   for i in range(2)]
            wuT = [pe(nc.sbuf_tensor(f"wuT{i}", [128, EC * 128], BF16))
                   for i in range(2)]
            sSG = [pe(nc.sbuf_tensor(f"sSG{i}", [128, QW], BF16)) for i in range(2)]
            sGT = xstA

            xnt0 = [BIG0[:, big0c - (i + 1) * E: big0c - i * E] for i in range(2)]
            tok_xnown, xn_tiles = ln_phase(xq_t, OWNC, xn_own, ("dX0", "dX1"),
                                           [tok_consts], "a",
                                           pool_ext=(xstA, xnt0))
            tok_b0T = P.cnt("sT")
            tok_b0V = P.cnt("sV")
            tok_b0S = P.cnt("sS")
            P.flush("b0")

            act_done = [[None] * NQW for _ in range(NHC)]
            wdT_done = [None] * NHC

            with ExitStack() as php1:
                pp = php1.enter_context
                Gps = [pp(nc.psum_tensor(f"Gps{i}", [128, QW], F32)) for i in range(3)]
                Ups = [pp(nc.psum_tensor(f"Ups{i}", [128, QW], F32)) for i in range(3)]

                f_gate = []
                wgc_free = [None, None]
                wuc_free = [None, None]
                wdc_free = [None, None]
                wgT_free = [None, None]
                wuT_free = [None, None]
                gps_free = [None] * 3
                ups_free = [None] * 3
                sg_free = [None, None]
                qctr = [0]

                def transpose_into(src, dst3, waits, par, extra_last=None):
                    ww = list(waits)
                    if extra_last is not None:
                        ww.append(extra_last)
                    tdt = P.op("SY", lambda e, o=dst3, i_=src[:]:
                               e.dma_start_transpose(out=o, in_=i_),
                               waits=ww, inc="dTW0" if par == 0 else "dTW1",
                               amt=16)
                    return [tdt] * (EC // GT)

                for hc in range(NHC):
                    par = hc % 2
                    dsem = "dW0" if par == 0 else "dW1"
                    tg = P.op("G", lambda g, o=wgc[par][:], s=wgr[hc]:
                              g.dma_start(out=o, in_=s),
                              waits=[wgc_free[par]] + (f_gate if hc < 2 else []),
                              inc=dsem, amt=16)
                    tu = P.op("G", lambda g, o=wuc[par][:], s=wur[hc]:
                              g.dma_start(out=o, in_=s),
                              waits=[wuc_free[par]], inc=dsem, amt=16)
                    td = P.op("G", lambda g,
                              o=wdc[0][:].rearrange("p (dc h) -> p dc h", dc=EC),
                              s=wdr[:, :, hc * 128:(hc + 1) * 128]:
                              g.dma_start(out=o, in_=s),
                              waits=[wdc_free[0]], inc=dsem, amt=16)
                    transpose_into(
                        wgc[par],
                        wgT[par][:].rearrange("p (c t) -> p c t", c=EC),
                        [td], par, extra_last=wgT_free[par])
                    transpose_into(
                        wuc[par],
                        wuT[par][:].rearrange("p (c t) -> p c t", c=EC),
                        [td], par, extra_last=wuT_free[par])
                    d_toks = transpose_into(
                        wdc[0],
                        wdT[:, hc, :].rearrange("p (c t) -> p c t", c=EC),
                        [td], par)
                    tall = d_toks[-1]
                    g_toks = [tall] * (EC // GT)
                    u_toks = [tall] * (EC // GT)
                    wgc_free[par] = tall
                    wuc_free[par] = tall
                    wdc_free[0] = tall
                    wdT_done[hc] = tall
                    last_g = None
                    last_u = None
                    for qs in range(NQW):
                        gp = qctr[0] % 3
                        qctr[0] += 1
                        for ec in range(EC):
                            last_g = P.op(
                                "T", lambda t, o=Gps[gp][:],
                                l=wgT[par][:, ec * 128:(ec + 1) * 128],
                                r=xn_own[:, ec, qs * QW:(qs + 1) * QW],
                                st=(ec == 0), sp=(ec == EC - 1):
                                t.matmul(o, l, r, start=st, stop=sp),
                                waits=[g_toks[ec // GT], gps_free[gp]] +
                                      xn_tiles[((qs + 1) * QW) // 128 - 1],
                                inc="sT" if ec == EC - 1 else None)
                        for ec in range(EC):
                            last_u = P.op(
                                "T", lambda t, o=Ups[gp][:],
                                l=wuT[par][:, ec * 128:(ec + 1) * 128],
                                r=xn_own[:, ec, qs * QW:(qs + 1) * QW],
                                st=(ec == 0), sp=(ec == EC - 1):
                                t.matmul(o, l, r, start=st, stop=sp),
                                waits=[u_toks[ec // GT], ups_free[gp]],
                                inc="sT" if ec == EC - 1 else None)
                        tsg = P.op("S", lambda s, o=sSG[qs % 2][:], i_=Gps[gp][:]:
                                   s.activation(o, i_, AF.Silu, scale=1.0 / 127.0),
                                   waits=[last_g, sg_free[qs % 2]], inc="sS")
                        gps_free[gp] = tsg
                        sp_ = qs % 2
                        tm = P.op("V", lambda v, o=ACT[:, hc, qs * QW:(qs + 1) * QW],
                                  a=sSG[sp_][:], b=Ups[gp][:]:
                                  v.tensor_tensor(o, a, b, ALU.mult),
                                  waits=[tsg, last_u] +
                                        ([tok_b0T] if hc == NHC - 1 else []),
                                  inc="sV")
                        ups_free[gp] = tm
                        sg_free[qs % 2] = tm
                        act_done[hc][qs] = tm
                    wgT_free[par] = last_g
                    wuT_free[par] = last_u

                tok_f1T = P.cnt("sT")
                tok_f1V = P.cnt("sV")
                tok_f1S = P.cnt("sS")
                P.flush("f1")

            with ExitStack() as php2:
                pp = php2.enter_context
                Dps = [pp(nc.psum_tensor(f"Dps{i}", [128, E], F32)) for i in range(4)]
                dps_free = [None] * 4
                gt_free = [tok_b0V, tok_b0V]
                ng_store = 0
                for q2 in range(OWNC):
                    dp = q2 % 4
                    qs = (q2 * 128) // QW
                    last = None
                    for ew in range(NEW):
                        for hc in range(NHC):
                            w = [act_done[hc][qs], wdT_done[hc], dps_free[dp]]
                            if q2 == 0 and ew == 0 and hc == 0:
                                w += [tok_f1V, tok_f1S]
                            last = P.op(
                                "T", lambda t, o=Dps[dp][:, ew * EW:(ew + 1) * EW],
                                l=ACT[:, hc, q2 * 128:(q2 + 1) * 128],
                                r=wdT[:, hc, ew * EW:(ew + 1) * EW],
                                st=(hc == 0), sp=(hc == NHC - 1):
                                t.matmul(o, l, r, start=st, stop=sp),
                                waits=w, inc="sT" if hc == NHC - 1 else None)
                    gp = q2 % 2
                    tsil = P.op("S", lambda s, o=sGT[gp][:], i_=Dps[dp][:]:
                                s.activation(o, i_, AF.Silu, scale=C2),
                                waits=[last, gt_free[gp]], inc="sS")
                    dps_free[dp] = tsil
                    tst = P.op("G", lambda g, o=gsp_t[q2], i_=sGT[gp][:]:
                               g.dma_start(out=o, in_=i_),
                               waits=[tsil], inc="dG0" if gp == 0 else "dG1",
                               amt=16)
                    gt_free[gp] = tst
                    ng_store += 1
                tok_f2T = P.cnt("sT")
                tok_f2S = P.cnt("sS")
                P.flush("f2")
            ph.close()

            with ExitStack() as phc:
                pc = phc.enter_context
                sPK = pc(nc.sbuf_tensor("sPK", [128, S], BF16))
                sTRI = pc(nc.sbuf_tensor("sTRI", [128, DGC * QBS], BF16))
                sPQ = Kp[1][:, 2 * E:2 * E + OWN]
                gdone = [("dG0", 16 * ((ng_store + 1) // 2)),
                         ("dG1", 16 * (ng_store // 2))]
                cgate = [tok_f1T, tok_f1V, tok_f2T, tok_f2S] + gdone
                P.op("G", lambda g: g.dma_start(out=sPQ, in_=d["posq"]),
                     waits=[tok_f2T], inc="dP", amt=16)
                tpq = ("dP", 16)
                P.op("G", lambda g: g.dma_start(out=sPK[:], in_=d["posk"]),
                     waits=cgate, inc="dK", amt=16)
                tpk = ("dK", 16)


                tok_vaug = [None, None]
                for i in range(2):
                    tok_vaug[i] = P.op("V", lambda v, i=i: v.memset(
                        Vaug[i][:, :, :, 64:65], 1.0),
                        waits=[tok_f2T, tok_f2S], inc="sV")

                SC2W = max(2 * QBS, 2 * GT * 128, 2 * QW)
                SC2 = [pc(nc.psum_tensor(f"SC2_{i}", [128, SC2W], F32))
                       for i in range(2)]
                AVPD = max(QBS, 512)
                KVPD = max(KW, 512)
                AV2 = pc(nc.psum_tensor("AV2", [128, max(2 * AVPD, E)], F32))
                KV2 = pc(nc.psum_tensor("KV2", [128, max(2 * KVPD, E)], F32))
                AVps = [AV2[0:65, 0:QBS], AV2[0:65, AVPD:AVPD + QBS]]
                KVp = [KV2[:, 0:KW], KV2[:, KVPD:KVPD + KW]]
                Qps = [SC2[0][:, 0:QW], AV2[:, 0:QW], AV2[:, QW:2 * QW]]
                SQp1 = [SC2[1][0:1, 0:QW], SC2[1][0:1, QW:2 * QW]]
                wq_free = [None, None]
                qps_free = [None, None, None]
                sqst = dict(free=[tok_f2T, tok_f2T])
                b4st = dict(sqmm_free=[tok_f2S, tok_f2S], qctr=0)
                tok_mq = [None] * NP
                qbs_per_qs = max(1, 4 // NQW)

                qst = [dict(), dict()]

                def q_dma(p):
                    par = p % 2
                    dsem = "dS0" if par == 0 else "dS1"
                    qst[par]["tw"] = P.op(
                        "G", lambda g, o=Wsl[par][:],
                        s=wqr[:, :, p * 128:(p + 1) * 128]:
                        g.dma_start(out=o, in_=s),
                        waits=[wq_free[par], tok_f2T], inc=dsem, amt=16)

                def q_qs(p, qs):
                    par = p % 2
                    qd = qst[par]
                    qp = b4st["qctr"] % 3
                    b4st["qctr"] += 1
                    last = None
                    for ec in range(EC):
                        w = [qd["tw"], qps_free[qp]] + tok_xnown
                        if p == 0 and ec == 0:
                            w += [tok_f2S, tok_f1V]
                        last = P.op(
                            "T", lambda t, o=Qps[qp][:],
                            l=Wsl[par][:, ec, :],
                            r=xn_own[:, ec, qs * QW:(qs + 1) * QW],
                            st=(ec == 0), sp=(ec == EC - 1):
                            t.matmul(o, l, r, start=st, stop=sp),
                            waits=w, inc="sT" if ec == EC - 1 else None)
                    wq_free[par] = last
                    tev = P.op("V", lambda v,
                               o=Q_all[:, p, qs * QW:(qs + 1) * QW],
                               a=Qps[qp][:],
                               b=sPQ[:, qs * QW:(qs + 1) * QW]:
                               v.tensor_tensor(o, a, b, ALU.add),
                               waits=[last, tpq], inc="sV")
                    qps_free[qp] = tev
                    qd[("tev", qs)] = tev

                def q_sq(p, qs):
                    par = p % 2
                    qd = qst[par]
                    sqi = qs % 2
                    tsq = P.op("V", lambda v, o=sSQ[sqi][:, 0:QW],
                               a=Q_all[:, p, qs * QW:(qs + 1) * QW]:
                               v.tensor_tensor(o, a, a, ALU.mult),
                               waits=[qd[("tev", qs)], sqst["free"][sqi]],
                               inc="sV")
                    tmm = P.op("T", lambda t, o=SQp1[sqi][0:1, 0:QW],
                               l=sONE[:], r=sSQ[sqi][:, 0:QW]:
                               t.matmul(o, l, r, start=True, stop=True),
                               waits=[tsq, tok_consts, tok_f1V,
                                      b4st["sqmm_free"][sqi]], inc="sT")
                    sqst["free"][sqi] = tmm
                    rlast = None
                    for jj in range(qbs_per_qs):
                        j = qs * qbs_per_qs + jj
                        rlast = P.op(
                            "V", lambda v, p=p, j=j, jj=jj, sqi=sqi:
                            v.reduce_max(
                                sMQ[0:1, p * 4 + j:p * 4 + j + 1],
                                SQp1[sqi][0:1, jj * QBS:(jj + 1) * QBS],
                                axis=AX.X),
                            waits=[tmm, rlast], inc="sV")
                    b4st["sqmm_free"][sqi] = rlast
                    if qs == NQW - 1:
                        tok_mq[p] = rlast

                def qpair_units(p):
                    units = []

                    def u0(p=p):
                        q_dma(p)
                        q_qs(p, 0)
                    units.append((3 * p, u0))
                    for k in range(1, NQW):
                        def uk(p=p, k=k):
                            q_qs(p, k)
                            if k >= 2:
                                q_sq(p, k - 2)
                        units.append((3 * p + (k + 1) // 2, uk))

                    def ulast(p=p):
                        for k in range(max(0, NQW - 2), NQW):
                            q_sq(p, k)
                    units.append((3 * p + 2, ulast))
                    return units

                st = dict(kread=[tok_f2T, tok_f2T],
                          vread=[tok_vaug[0], tok_vaug[1]],
                          bm_free=[None, None],
                          sc2=[None, None],
                          kvb=[tok_f2S, tok_f2S],
                          av_free=[None, None],
                          expb_free=[tok_f2T] * 2,
                          sl_free=[None] * 4, bm_tok=[None, None],
                          k_tok=[None, None], v_tok=[None, None],
                          rcp_free=[None] * 4, rcf_free=[None, None],
                          rs_free=tok_f2T)
                kvst = [dict(), dict()]
                TPK = KW // 128

                def kv_units(p, ks, vs, xw):
                    par = p % 2
                    dsem = "dV0" if par == 0 else "dV1"
                    kv = kvst[par]
                    units = []

                    def u_dma():
                        kv["rm_tok2"] = None
                        kv["twk"] = P.op("G", lambda g, o=Wsl[ks][:],
                                         s=wkr[:, :, p * 128:(p + 1) * 128]:
                                         g.dma_start(out=o, in_=s),
                                         waits=[st["sl_free"][ks], tok_f2T],
                                         inc=dsem, amt=16)
                        kv["twv"] = P.op("G", lambda g, o=Wsl[vs][:],
                                         s=wvr[:, :, p * 128:(p + 1) * 128]:
                                         g.dma_start(out=o, in_=s),
                                         waits=[st["sl_free"][vs], tok_f2T],
                                         inc=dsem, amt=16)
                    units.append((0, u_dma))

                    def emit_kg(kg):
                        bi = kg % 2
                        bank = KVp[bi]
                        last = None
                        xt = xw((kg + 1) * TPK - 1)
                        for ec in range(EC):
                            w = [kv["twv"], st["kvb"][bi]] + xt
                            if ec == 0:
                                w.append(tok_f1V)
                            last = P.op(
                                "T", lambda t, o=bank[:, 0:KW],
                                l=Wsl[ks][:, ec, :],
                                r=xn_kv[:, ec, kg * KW:(kg + 1) * KW],
                                st_=(ec == 0), sp=(ec == EC - 1):
                                t.matmul(o, l, r, start=st_, stop=sp),
                                waits=w,
                                inc="sT" if ec == EC - 1 else None)
                        tev = P.op("V", lambda v,
                                   o=Kp[par][:, kg * KW:(kg + 1) * KW],
                                   a=bank[:, 0:KW],
                                   b=sPK[:, kg * KW:(kg + 1) * KW]:
                                   v.tensor_tensor(o, a, b, ALU.add),
                                   waits=[last, tpk, st["kread"][par]],
                                   inc="sV")
                        st["kvb"][bi] = tev
                        kv["lastk"] = last
                        kv["ktok"] = tev

                    for kg_ in range(NKG):
                        units.append(((kg_ + 1) * TPK - 1,
                                      lambda kg_=kg_: emit_kg(kg_)))

                    def u_setk():
                        st["sl_free"][ks] = kv["lastk"]
                        st["k_tok"][par] = kv["ktok"]
                    units.append((NKG * TPK - 1, u_setk))

                    def u_norm1(kg):
                        sqi = kg % 2
                        tsq = P.op("V", lambda v, sqi=sqi,
                                   o=sSQ[sqi][:, 0:KW],
                                   a=Kp[par][:, kg * KW:(kg + 1) * KW]:
                                   v.tensor_tensor(o, a, a, ALU.mult),
                                   waits=[st["kvb"][kg % 2],
                                          sqst["free"][sqi], kv["ktok"]],
                                   inc="sV")
                        bi = kg % 2
                        tmm = P.op("T", lambda t, bi=bi, sqi=sqi,
                                   o=KVp[bi][0:1, 0:KW]:
                                   t.matmul(o, sONE[:], sSQ[sqi][:, 0:KW],
                                            start=True, stop=True),
                                   waits=[tsq], inc="sT")
                        sqst["free"][sqi] = tmm
                        col = 1 if kg == 0 else 2
                        trd = P.op("V", lambda v, col=col, bi=bi:
                                   v.reduce_max(sMK[0:1, col:col + 1],
                                                KVp[bi][0:1, 0:KW],
                                                axis=AX.X),
                                   waits=[tmm, kv.get("rm_tok2")], inc="sV")
                        st["kvb"][bi] = trd
                        if kg == 0:
                            kv["rm_tok2"] = P.op(
                                "V", lambda v: v.tensor_copy(
                                    sMK[0:1, 0:1], sMK[0:1, 1:2]),
                                waits=[trd], inc="sV")
                        else:
                            kv["rm_tok2"] = P.op(
                                "V", lambda v: v.tensor_tensor(
                                    sMK[0:1, 0:1], sMK[0:1, 0:1],
                                    sMK[0:1, 2:3], ALU.max),
                                waits=[trd, kv["rm_tok2"]], inc="sV")
                        kv["rm_tok"] = kv["rm_tok2"]

                    for kg_ in range(NKG):
                        units.append(((kg_ + 1) * TPK - 1,
                                      lambda kg_=kg_: u_norm1(kg_)))

                    def emit_vc(c):
                        bi = c % 2
                        bank = KVp[bi]
                        last = None
                        xt = xw(c)
                        for ec in range(EC):
                            w = [kv["twv"], st["kvb"][bi]] + xt
                            if ec == 0:
                                w.append(tok_f1V)
                            last = P.op(
                                "T", lambda t, o=bank[:, 0:128],
                                l=xn_kv[:, ec, c * 128:(c + 1) * 128],
                                r=Wsl[vs][:, ec, :],
                                st_=(ec == 0), sp=(ec == EC - 1):
                                t.matmul(o, l, r, start=st_, stop=sp),
                                waits=w,
                                inc="sT" if ec == EC - 1 else None)
                        tev = P.op("S", lambda s_,
                                   o=Vaug[par][:, c, :, 0:64],
                                   i_=bank[:, 0:128].rearrange(
                                       "p (two sv) -> p two sv", two=2):
                                   s_.copy(o, i_),
                                   waits=[last, st["vread"][par]],
                                   inc="sS")
                        st["kvb"][bi] = tev
                        if c == KVC - 1:
                            st["sl_free"][vs] = last
                        kv["lastv"] = tev

                    for c_ in range(KVC):
                        units.append((c_, lambda c_=c_: emit_vc(c_)))

                    def u_bound():
                        st["v_tok"][par] = kv["lastv"]
                        t1 = P.op("V", lambda v, p=p: v.tensor_scalar(
                            sMK[0:1, 4:8], sMQ[0:1, p * 4:p * 4 + 4],
                            sMK[0:1, 0:1], 0.5, ALU.add, ALU.mult),
                            waits=[kv["rm_tok"], tok_mq[p]], inc="sV")
                        t2 = P.op("V", lambda v: v.tensor_scalar(
                            sMK[0:1, 4:8], sMK[0:1, 4:8], SCLAMP, -cfg.SM,
                            ALU.min, ALU.mult), waits=[t1], inc="sV")
                        t2b = P.op("V", lambda v: v.tensor_scalar(
                            sMK[0:1, 4:8], sMK[0:1, 4:8], 12.0, None,
                            ALU.add), waits=[t2], inc="sV")
                        t3 = P.op("V", lambda v: v.tensor_copy(
                            sBSB[:], sMK[0:1, 4:8]), waits=[t2b], inc="sV")
                        lastb = None
                        for j in range(4):
                            lastb = P.op(
                                "T", lambda t, j=j:
                                t.matmul(KVp[0][:, j:j + 1], sOR[0:1, :],
                                         sBSB[0:1, j:j + 1],
                                         start=True, stop=True),
                                waits=[t3, tok_consts, st["kvb"][0]],
                                inc="sT" if j == 3 else None)
                        bm_tok = None
                        for j in range(4):
                            bm_tok = P.op(
                                "V", lambda v, j=j, par=par:
                                v.tensor_scalar(
                                    sBM[par][:, j * KVC:(j + 1) * KVC],
                                    sWL[:, j * KVC:(j + 1) * KVC],
                                    KVp[0][:, j:j + 1], None, ALU.add),
                                waits=[lastb, tok_consts,
                                       st["bm_free"][par]],
                                inc="sV")
                        st["bm_tok"][par] = bm_tok
                        st["kvb"][0] = bm_tok
                    units.append((KVC - 1, u_bound))
                    return units

                ntile_kv = S // 128
                b34q = []
                for p_ in range(NP):
                    b34q += qpair_units(p_)
                holder = {}
                b34q += kv_units(0, 2, 3, lambda r: holder["toks"][r])
                b34q.sort(key=lambda it: it[0])
                qpos = [0]

                def b34_cb(n, toks):
                    holder["toks"] = toks
                    c = 0
                    while (qpos[0] < len(b34q) and b34q[qpos[0]][0] <= n
                           and c < 3):
                        b34q[qpos[0]][1]()
                        qpos[0] += 1
                        c += 1

                tok_xnkv, _ = ln_phase(
                    xkv_t, ntile_kv, xn_kv, ("dX0", "dX1"),
                    [tok_f1T, tok_f1V], "b",
                    xnt_ext=[Kp[1][:, i * E:(i + 1) * E]
                             for i in range(2)],
                    xnt_gate=tok_f2T,
                    dt_gate=tok_f2T, sq_gate=gdone,
                    sq_ext=[xstA[0], xstA[1]],
                    interleave_cb=b34_cb)
                while qpos[0] < len(b34q):
                    b34q[qpos[0]][1]()
                    qpos[0] += 1
                tok_b4T = P.cnt("sT")
                tok_b4V = P.cnt("sV")
                tok_b4S = P.cnt("sS")
                P.flush("b34")

                tok_Q = tok_b4V
                st["sc2"] = [tok_Q, tok_b4S]
                st["av_free"] = [tok_Q, tok_Q]
                st["sl_free"][0] = tok_b4T
                st["sl_free"][1] = tok_b4T

                with ExitStack() as ph5:
                    p5 = ph5.enter_context
                    sRCP = [p5(nc.sbuf_tensor(f"sRCP{i}", [1, QBS], BF16))
                            for i in range(4)]
                    sRCF = [p5(nc.sbuf_tensor(f"sRCF{i}", [1, QBS], F32))
                            for i in range(2)]
                    ttri = P.op("SY", ld(sTRI[:], d["tri"]),
                                waits=[tok_f1T, tok_f1V], inc="dC", amt=16)
                    fillers = []
                    pend = [None]

                    def attention(p):
                        par = p % 2
                        for j in range(4):
                            ext = cfg.EXT[j]
                            avdep = [None, None]
                            ev_tok = {}

                            def emit_sc(c):
                                bi = c % 2
                                lm = None
                                for h in range(2):
                                    lm = P.op(
                                        "T", lambda t, bi=bi, h=h, c=c, j=j, par=par,
                                        p=p:
                                        t.matmul(
                                            SC2[bi][:, h * QBS:(h + 1) * QBS],
                                            Kp[par][h * 64:(h + 1) * 64,
                                                    c * 128:(c + 1) * 128],
                                            Q_all[h * 64:(h + 1) * 64, p,
                                                  j * QBS:(j + 1) * QBS],
                                            start=True, stop=True),
                                        waits=[st["k_tok"][par], tok_Q,
                                               st["sc2"][bi]],
                                        inc="sT" if h == 1 else None)
                                st["kread"][par] = lm
                                col = j * KVC + c
                                te = P.op(
                                    "S", lambda s, bi=bi, col=col, par=par:
                                    s.activation(expb[bi],
                                                 SC2[bi][:, 0:2 * QBS], AF.Exp,
                                                 scale=cfg.SM,
                                                 bias=sBM[par][:, col:col + 1]),
                                    waits=[lm, st["bm_tok"][par],
                                           st["expb_free"][bi]],
                                    inc="sS")
                                st["sc2"][bi] = te
                                ev = te
                                if c >= ext - DGC:
                                    r = c - (ext - DGC)
                                    for h in range(2):
                                        ev = P.op(
                                            "V", lambda v, bi=bi, r=r, h=h:
                                            v.tensor_tensor(
                                                expb[bi][:, h * QBS:(h + 1) * QBS],
                                                expb[bi][:, h * QBS:(h + 1) * QBS],
                                                sTRI[:, r * QBS:(r + 1) * QBS],
                                                ALU.mult),
                                            waits=[te, ttri], inc="sV")
                                ev_tok[c] = ev

                            def emit_av(c):
                                bi = c % 2
                                lm = None
                                for h in range(2):
                                    w = [ev_tok.pop(c) if h == 1 else ev_tok[c],
                                         st["v_tok"][par]]
                                    if c == 0:
                                        w.append(st["av_free"][h])
                                    lm = P.op(
                                        "T", lambda t, h=h, bi=bi, c=c, par=par,
                                        st_=(c == 0), sp=(c == ext - 1):
                                        t.matmul(AVps[h][:], Vaug[par][:, c, h, :],
                                                 expb[bi][:, h * QBS:(h + 1) * QBS],
                                                 start=st_, stop=sp),
                                        waits=w, inc="sT" if h == 1 else None)
                                if c == ext - 1:
                                    avdep[0] = lm
                                    avdep[1] = lm
                                st["vread"][par] = lm
                                st["expb_free"][bi] = lm

                            for c in range(ext):
                                if c >= 2:
                                    emit_av(c - 2)
                                emit_sc(c)
                                if c == 1 and pend[0] is not None:
                                    pend[0]()
                                    pend[0] = None
                                if c % 2 == 1 and fillers:
                                    fillers.pop(0)()
                            for c in range(max(0, ext - 2), ext):
                                emit_av(c)
                            if fillers:
                                fillers.pop(0)()

                            def fin(j=j, p=p, par=par, avdep=avdep):
                                rp = 2 * (j % 2)
                                tcp = [None, None]
                                tbc = [None, None]
                                for h in range(2):
                                    tl1 = P.op("S", lambda s, h=h:
                                               s.activation(sRCF[h][0:1, :],
                                                            AVps[h][64:65, :],
                                                            AF.Ln),
                                               waits=[avdep[h],
                                                      st["rcf_free"][h],
                                                      st["rcp_free"][rp + h],
                                                      tok_f1V, tok_f2S] + gdone,
                                               inc="sS")
                                    tl2 = P.op("S", lambda s, h=h, rp=rp:
                                               s.activation(sRCP[rp + h][0:1, :],
                                                            sRCF[h][0:1, :],
                                                            AF.Exp, scale=-1.0),
                                               waits=[tl1], inc="sS")
                                    st["rcf_free"][h] = tl2
                                    tbc[h] = P.op(
                                        "T", lambda t, h=h, rp=rp:
                                        t.matmul(KVp[h][0:64, 0:QBS],
                                                 sOR[0:1, 0:64],
                                                 sRCP[rp + h][0:1, :],
                                                 start=True, stop=True),
                                        waits=[tl2, tok_consts, st["kvb"][h]],
                                        inc="sT" if h == 1 else None)
                                    st["rcp_free"][rp + h] = tbc[h]
                                for h in range(2):
                                    tcp[h] = P.op(
                                        "V", lambda v, h=h:
                                        v.tensor_copy(sRS[h * 64:(h + 1) * 64, :],
                                                      KVp[h][0:64, 0:QBS]),
                                        waits=[tbc[1], st["rs_free"]], inc="sV")
                                st["kvb"][0] = tcp[1]
                                st["kvb"][1] = tcp[1]
                                for h in range(2):
                                    tdv = P.op(
                                        "V", lambda v, h=h, p=p, j=j:
                                        v.tensor_tensor(
                                            AO[h * 64:(h + 1) * 64, p,
                                               j * QBS:(j + 1) * QBS],
                                            AVps[h][0:64, :],
                                            sRS[h * 64:(h + 1) * 64, :],
                                            ALU.mult),
                                        waits=[tcp[1], avdep[h], tok_b4T],
                                        inc="sV")
                                    st["av_free"][h] = tdv
                                    st["rs_free"] = tdv
                            pend[0] = fin
                        st["bm_free"][par] = P.cnt("sS")
                        while fillers:
                            fillers.pop(0)()

                    for p in range(NP):
                        if p + 1 < NP:
                            fillers.extend(
                                fn for _, fn in kv_units(
                                    p + 1, (p + 1) % 2, 2 + (p + 1) % 2,
                                    lambda r: tok_xnkv))
                        attention(p)
                    if pend[0] is not None:
                        pend[0]()
                        pend[0] = None
                    two = P.op("G", lambda g: g.dma_start(
                        out=BIG0[:, 0:EC * E].rearrange("p (rc e) -> p rc e", rc=EC),
                        in_=wor),
                               waits=[st["k_tok"][(NP - 1) % 2],
                                      st["v_tok"][(NP - 1) % 2]],
                               inc="dS0", amt=16)
                    tok_b5T = P.cnt("sT")
                    tok_b5V = P.cnt("sV")
                    tok_b5S = P.cnt("sS")
                    P.flush("b5")

            if True:
                with ExitStack() as ph6:
                    p6 = ph6.enter_context
                    OPs = [p6(nc.psum_tensor(f"OPs{i}", [128, E], F32))
                           for i in range(4)]
                    gt6 = [p6(nc.sbuf_tensor(f"gt6{i}", [128, E], F32))
                           for i in range(2)]
                    xt6 = xstA
                    ot6 = [p6(nc.sbuf_tensor(f"ot6{i}", [128, E], F32))
                           for i in range(2)]

                    gates6 = [tok_b5T, tok_b5V, tok_b5S,
                              ("dG0", 16 * ((ng_store + 1) // 2)),
                              ("dG1", 16 * (ng_store // 2))]
                    ops_free = [None] * 4
                    g_free = [None, None]
                    x_free = [None, None]
                    o_free = [None, None]
                    t_free = None
                    nout = 0
                    for q2 in range(OWNC):
                        par = q2 % 2
                        dsem = "dR0" if par == 0 else "dR1"
                        tgl = P.op("G", lambda g, o=gt6[par][:], s=gsp_t[q2]:
                                   g.dma_start(out=o, in_=s),
                                   waits=[g_free[par]] + (gates6 if q2 < 2 else []),
                                   inc="dR2" if par == 0 else "dR3", amt=16)
                        txl = P.op("SY", lambda sy, o=xt6[par][:], s=xq_t[q2]:
                                   sy.dma_start(out=o, in_=s),
                                   waits=[x_free[par]] +
                                         (gates6 if q2 < 2 else []),
                                   inc=dsem, amt=16)
                        op = q2 % 4
                        last = None
                        for ew in range(NEW):
                            for rc in range(NP):
                                w = [ops_free[op]]
                                if q2 == 0 and ew == 0 and rc == 0:
                                    w += [tok_b5V, tok_b5S]
                                w += [tok_b5V, two]
                                last = P.op(
                                    "T", lambda t, op=op, ew=ew, rc=rc, q2=q2,
                                    st_=(rc == 0), sp=(rc == NP - 1):
                                    t.matmul(OPs[op][:, ew * EW:(ew + 1) * EW],
                                             AO[:, rc, q2 * 128:(q2 + 1) * 128],
                                             Wo_sb[:, rc, ew * EW:(ew + 1) * EW],
                                             start=st_, stop=sp),
                                    waits=w, inc="sT" if rc == NP - 1 else None)
                        lev = None
                        for ew in range(NEW):
                            tmul = P.op("V", lambda v, op=op, ew=ew, par=par:
                                        v.tensor_tensor(
                                            ot6[par][:, ew * EW:(ew + 1) * EW],
                                            OPs[op][:, ew * EW:(ew + 1) * EW],
                                            gt6[par][:, ew * EW:(ew + 1) * EW],
                                            ALU.mult),
                                        waits=[last, txl, tgl, t_free,
                                               o_free[par]],
                                        inc="sV")
                            lev = P.op("V", lambda v, op=op, ew=ew, par=par:
                                       v.tensor_tensor(
                                           ot6[par][:, ew * EW:(ew + 1) * EW],
                                           ot6[par][:, ew * EW:(ew + 1) * EW],
                                           xt6[par][:, ew * EW:(ew + 1) * EW],
                                           ALU.add),
                                       waits=[tmul], inc="sV")
                            t_free = lev
                        ops_free[op] = lev
                        g_free[par] = lev
                        x_free[par] = lev
                        tst = P.op("SY", lambda sy, o=out_t[q2], i_=ot6[par][:]:
                                   sy.dma_start(out=o, in_=i_),
                                   waits=[lev], inc="dO0" if par == 0 else "dO1",
                                   amt=16)
                        o_free[par] = tst
                        nout += 1
                    P.op("SY", lambda sy: sy.nop(),
                         waits=[("dO0", 16 * ((nout + 1) // 2)),
                                ("dO1", 16 * (nout // 2))])
                    P.flush("b6")

    return nc



def kernel(L0_attn_x, L0_attn_Wq, L0_attn_Wk, L0_attn_Wv, L0_attn_Wo,
           L0_ffn_w_gate, L0_ffn_w_up, L0_ffn_w_down):
    cfg = FULL
    x = np.asarray(L0_attn_x, dtype=np.float32)
    maps, owns = host_inputs(
        cfg, x,
        np.asarray(L0_attn_Wq, np.float32), np.asarray(L0_attn_Wk, np.float32),
        np.asarray(L0_attn_Wv, np.float32), np.asarray(L0_attn_Wo, np.float32),
        np.asarray(L0_ffn_w_gate, np.int32), np.asarray(L0_ffn_w_up, np.int32),
        np.asarray(L0_ffn_w_down, np.int32))
    nc = build_program(cfg)
    res = run_bass_kernel_spmd(nc, maps, core_ids=list(range(cfg.ncores)))
    out = np.empty((cfg.NB, cfg.S, cfg.E), np.float32)
    for c, (b, own) in enumerate(owns):
        out[b][own] = np.asarray(res.results[c]["out"], np.float32)
    return out


# version 24
# speedup vs baseline: 1.0213x; 1.0213x over previous
import math
import numpy as np
import ml_dtypes
from contextlib import ExitStack

import concourse.bass as bass
import concourse.mybir as mybir
from concourse.bass_utils import run_bass_kernel_spmd

F32 = mybir.dt.float32
BF16 = mybir.dt.bfloat16
I32 = mybir.dt.int32
AF = mybir.ActivationFunctionType
ALU = mybir.AluOpType
AX = mybir.AxisListType

NEG = -1.0e30
SCLAMP = 1280.0


class Cfg:
    def __init__(self, S=4096, E=1024, NH=16, HID=3072, NB=4):
        self.S, self.E, self.NH, self.HID, self.NB = S, E, NH, HID, NB
        self.DH = 64
        self.NP = NH // 2
        self.EC = E // 128
        self.KVC = S // 128
        self.QBS = S // 8
        self.DGC = self.QBS // 128
        self.OWN = S // 2
        self.OWNC = self.OWN // 128
        self.NHC = HID // 128
        self.EXT = [(j + 1) * 2 * self.DGC for j in range(4)]
        self.QW = min(512, self.OWN)
        self.NQW = self.OWN // self.QW
        self.KW = min(512, S, 2 * self.QBS)
        self.NKG = S // self.KW
        self.EW = min(512, E)
        self.NEW = E // self.EW
        self.GT = min(4, self.EC)
        self.SM = 1.0 / math.sqrt(E)
        self.ncores = 2 * NB


FULL = Cfg()



def core_layout(cfg, t):
    D = cfg.DGC
    own = np.concatenate([np.arange(cfg.QBS) + cfg.QBS * (2 * j + t)
                          for j in range(4)])
    perm = np.arange(cfg.KVC)
    if t == 0:
        perm = perm.reshape(-1, 2, D)[:, ::-1, :].reshape(-1)
    kvrows = (perm[:, None] * 128 + np.arange(128)[None, :]).reshape(-1)
    return own, perm, kvrows


def wl_table(cfg, t):
    D = cfg.DGC
    wl = np.zeros((4, cfg.KVC), np.float32)
    for j in range(4):
        ext = cfg.EXT[j]
        for c in range(cfg.KVC):
            if c >= ext:
                wl[j, c] = NEG
            elif c >= ext - D:
                wl[j, c] = 0.0
            elif t == 0 and 2 * D * j <= c < 2 * D * j + D:
                wl[j, c] = NEG
    return wl


def pos_table(cfg):
    EPS = float(np.finfo(np.float32).eps)
    idx = np.arange(cfg.S, dtype=np.float64)
    scale = 1.0 / (10000.0 ** (np.arange(0, 64, 2, dtype=np.float64) / 64.0) + EPS)
    ang = idx[:, None] * scale[None, :]
    pos = np.stack([np.sin(ang), np.cos(ang)], axis=-1).reshape(cfg.S, 64)
    return pos.astype(np.float32)


def host_inputs(cfg, x, Wq, Wk, Wv, Wo, wg, wu, wd):
    bf = ml_dtypes.bfloat16
    pos = pos_table(cfg)
    ident = np.eye(128, dtype=bf)
    tri = np.zeros((128, cfg.DGC * cfg.QBS), np.float32)
    for r in range(cfg.DGC):
        k = 128 * r + np.arange(128)[:, None]
        q = np.arange(cfg.QBS)[None, :]
        tri[:, r * cfg.QBS:(r + 1) * cfg.QBS] = (k <= q)
    tri = tri.astype(bf)
    onescol = np.ones((128, 1), dtype=bf)
    onesrow = np.ones((1, 128), dtype=bf)
    cvec = np.zeros((128, 4), np.float32)
    cvec[:, 0] = 1e-5

    shared = dict(
        ident=ident, tri=tri, onescol=onescol, onesrow=onesrow, cvec=cvec,
        Wq=np.ascontiguousarray(Wq), Wk=np.ascontiguousarray(Wk),
        Wv=np.ascontiguousarray(Wv), Wo=np.ascontiguousarray(Wo),
        wg=np.ascontiguousarray(wg), wu=np.ascontiguousarray(wu),
        wd=np.ascontiguousarray(wd),
    )
    maps, owns = [], []
    for c in range(cfg.ncores):
        b, t = c // 2, c % 2
        own, perm, kvrows = core_layout(cfg, t)
        wl = wl_table(cfg, t)
        WL = np.repeat(wl.reshape(1, -1), 128, axis=0).astype(np.float32)
        pq = pos[own].T
        pk = pos[kvrows].T
        m = dict(shared)
        m["x_q"] = np.ascontiguousarray(x[b][own])
        m["x_kv"] = np.ascontiguousarray(x[b][kvrows])
        m["posq"] = np.ascontiguousarray(np.concatenate([pq, pq], 0).astype(np.float32))
        m["posk"] = np.ascontiguousarray(np.concatenate([pk, pk], 0).astype(np.float32))
        m["WL"] = WL
        maps.append(m)
        owns.append((b, own))
    return maps, owns



class PB:
    ENG = ("SY", "G", "T", "V", "S")

    def __init__(self, nc):
        self.nc = nc
        self.q = {k: [] for k in self.ENG}
        self.sems = {}
        self.hw = {k: {} for k in self.ENG}

    def mksem(self, stack, *names):
        for n in names:
            self.sems[n] = [stack.enter_context(self.nc.semaphore(n)), 0]

    def cnt(self, name):
        return (name, self.sems[name][1])

    def op(self, qn, fn, waits=(), inc=None, amt=1):
        w = [x for x in waits if x is not None]
        self.q[qn].append((fn, w, inc, amt))
        if inc is not None:
            self.sems[inc][1] += amt
            return (inc, self.sems[inc][1])
        return None

    def flush(self, name):
        def runner(qn, ops):
            def body(eng):
                hw = self.hw[qn]
                for fn, waits, inc, amt in ops:
                    for (sn, val) in waits:
                        if val > hw.get(sn, 0):
                            eng.wait_ge(self.sems[sn][0], val)
                            hw[sn] = val
                    ins = fn(eng)
                    if inc is not None:
                        ins.then_inc(self.sems[inc][0], amt)
            return body

        with self.nc.Block(name) as block:
            tgt = dict(SY=block.sync, G=block.gpsimd, T=block.tensor,
                       V=block.vector, S=block.scalar)
            for qn in self.ENG:
                if self.q[qn]:
                    tgt[qn](runner(qn, self.q[qn]))
                    self.q[qn] = []


def build_program(cfg):
    nc = bass.Bass("TRN2", target_bir_lowering=False)
    S, E, NP, EC, KVC = cfg.S, cfg.E, cfg.NP, cfg.EC, cfg.KVC
    QBS, DGC, OWN, OWNC = cfg.QBS, cfg.DGC, cfg.OWN, cfg.OWNC
    NHC, QW, NQW, KW, NKG = cfg.NHC, cfg.QW, cfg.NQW, cfg.KW, cfg.NKG
    GT, EW, NEW = cfg.GT, cfg.EW, cfg.NEW
    C2 = 1.0 / (127.0 * 127.0)

    d = {}
    d["x_q"] = nc.dram_tensor("x_q", [OWN, E], F32, kind="ExternalInput").ap()
    d["x_kv"] = nc.dram_tensor("x_kv", [S, E], F32, kind="ExternalInput").ap()
    d["posq"] = nc.dram_tensor("posq", [128, OWN], F32, kind="ExternalInput").ap()
    d["posk"] = nc.dram_tensor("posk", [128, S], F32, kind="ExternalInput").ap()
    d["WL"] = nc.dram_tensor("WL", [128, 4 * KVC], F32, kind="ExternalInput").ap()
    d["tri"] = nc.dram_tensor("tri", [128, DGC * QBS], BF16, kind="ExternalInput").ap()
    d["ident"] = nc.dram_tensor("ident", [128, 128], BF16, kind="ExternalInput").ap()
    d["onescol"] = nc.dram_tensor("onescol", [128, 1], BF16, kind="ExternalInput").ap()
    d["onesrow"] = nc.dram_tensor("onesrow", [1, 128], BF16, kind="ExternalInput").ap()
    d["cvec"] = nc.dram_tensor("cvec", [128, 4], F32, kind="ExternalInput").ap()
    for w in ("Wq", "Wk", "Wv", "Wo"):
        d[w] = nc.dram_tensor(w, [E, E], F32, kind="ExternalInput").ap()
    d["wg"] = nc.dram_tensor("wg", [cfg.HID, E], I32, kind="ExternalInput").ap()
    d["wu"] = nc.dram_tensor("wu", [cfg.HID, E], I32, kind="ExternalInput").ap()
    d["wd"] = nc.dram_tensor("wd", [E, cfg.HID], I32, kind="ExternalInput").ap()
    d["out"] = nc.dram_tensor("out", [OWN, E], F32, kind="ExternalOutput").ap()
    gsp = nc.dram_tensor("g_spill", [OWN, E], F32, kind="Internal").ap()

    xq_t = d["x_q"].rearrange("(n p) e -> n p e", p=128)
    xkv_t = d["x_kv"].rearrange("(n p) e -> n p e", p=128)
    out_t = d["out"].rearrange("(n p) e -> n p e", p=128)
    gsp_t = gsp.rearrange("(n p) e -> n p e", p=128)
    wgr = d["wg"].rearrange("(h p) e -> h p e", p=128)
    wur = d["wu"].rearrange("(h p) e -> h p e", p=128)
    wdr = d["wd"].rearrange("(dc p) h -> p dc h", p=128)
    wqr = d["Wq"].rearrange("(ec p) e -> p ec e", p=128)
    wkr = d["Wk"].rearrange("(ec p) e -> p ec e", p=128)
    wvr = d["Wv"].rearrange("(ec p) e -> p ec e", p=128)
    wor = d["Wo"].rearrange("(rc p) e -> p rc e", p=128)

    P = PB(nc)

    with ExitStack() as top:
        P.mksem(top, "sT", "sS", "sV", "dC", "dX0", "dX1", "dW0", "dW1",
                "dS0", "dS1", "dV0", "dV1", "dT0", "dT1", "dTW0", "dTW1", "dG0", "dG1", "dR0", "dR1", "dR2", "dR3", "dO0", "dO1", "dP", "dK")
        e = top.enter_context

        sWL = e(nc.sbuf_tensor("sWL", [128, 4 * KVC], F32))
        sBM = [e(nc.sbuf_tensor(f"sBM{i}", [128, 4 * KVC], F32)) for i in range(2)]
        sID = e(nc.sbuf_tensor("sID", [128, 128], BF16))
        sONE = e(nc.sbuf_tensor("sONE", [128, 1], BF16))
        sOR = e(nc.sbuf_tensor("sOR", [1, 128], BF16))
        sCV = e(nc.sbuf_tensor("sCV", [128, 4], F32))
        RA = e(nc.sbuf_tensor("RA", [128, max(EC, NP) * OWN], BF16))
        big0c = max(NHC * OWN, EC * S + NP * OWN, EC * E)
        BIG0 = e(nc.sbuf_tensor("BIG0", [128, big0c], BF16))
        b5set = 2 * S + 2 * KVC * 130 + 4 * EC * 128 + 2 * 2 * QBS + 1024 + QBS
        big1c = max(NHC * E, b5set)
        BIG1 = e(nc.sbuf_tensor("BIG1", [128, big1c], BF16))
        sLN = [e(nc.sbuf_tensor(f"sLN{i}", [128, 8], F32)) for i in range(2)]
        sMQ = e(nc.sbuf_tensor("sMQ", [1, NP * 4], F32))
        sMK = e(nc.sbuf_tensor("sMK", [1, 8], F32))
        sBSB = e(nc.sbuf_tensor("sBSB", [1, 4], BF16))


        xn_own = RA[:, 0:EC * OWN].rearrange("p (ec t) -> p ec t", ec=EC)
        AO = RA[:, 0:NP * OWN].rearrange("p (np t) -> p np t", np=NP)
        ACT = BIG0[:, 0:NHC * OWN].rearrange("p (hc t) -> p hc t", hc=NHC)
        xn_kv = BIG0[:, 0:EC * S].rearrange("p (ec t) -> p ec t", ec=EC)
        Q_all = BIG0[:, EC * S:EC * S + NP * OWN].rearrange("p (np t) -> p np t", np=NP)
        Wo_sb = BIG0[:, 0:EC * E].rearrange("p (rc e) -> p rc e", rc=EC)
        wdT = BIG1[:, 0:NHC * E].rearrange("p (hc e) -> p hc e", hc=NHC)
        o = 0
        Kp = [BIG1[:, o + i * S: o + (i + 1) * S] for i in range(2)]
        o += 2 * S
        Vaug = [BIG1[:, o + i * KVC * 130: o + (i + 1) * KVC * 130]
                .rearrange("p (c two sv) -> p c two sv", two=2, sv=65) for i in range(2)]
        o += 2 * KVC * 130
        Wsl = [BIG1[:, o + i * EC * 128: o + (i + 1) * EC * 128]
               .rearrange("p (ec dd) -> p ec dd", ec=EC) for i in range(4)]
        o += 4 * EC * 128
        expb = [BIG1[:, o + i * 2 * QBS: o + (i + 1) * 2 * QBS] for i in range(2)]
        o += 2 * 2 * QBS
        sSQ = [BIG1[:, o:o + 512], BIG1[:, o + 512:o + 1024]]
        o += 1024
        sRS = BIG1[:, o:o + QBS]

        def ld(dst, src):
            return lambda sy, dst=dst, src=src: sy.dma_start(out=dst, in_=src)

        nconst = 0
        for dst, src in ((sWL[:], d["WL"]), (sID[:], d["ident"]),
                         (sONE[:], d["onescol"]),
                         (sOR[:], d["onesrow"]), (sCV[:], d["cvec"])):
            P.op("SY", ld(dst, src), inc="dC", amt=16)
            nconst += 1
        tok_consts = ("dC", 16 * nconst)

        def ln_phase(xt_src, ntile, xn_dst, dsem, gates, tag, pool_ext=None,
                     xnt_ext=None, xnt_gate=None,
                     dt_gate=None, sq_gate=None, interleave_cb=None,
                     sq_ext=None):
            with ExitStack() as ph:
                pe = ph.enter_context
                if pool_ext is not None:
                    xst, xnt = pool_ext
                else:
                    xst = [pe(nc.sbuf_tensor(f"xst{tag}{i}", [128, E], F32))
                           for i in range(2)]
                    xnt = xnt_ext if xnt_ext is not None else [
                        pe(nc.sbuf_tensor(f"xnt{tag}{i}", [128, E], BF16))
                        for i in range(2)]
                sq = sq_ext if sq_ext is not None else [
                    pe(nc.psum_tensor(f"sqb{tag}{i}", [128, E], F32))
                    for i in range(2)]
                xst_free = [None, None]
                sq_free = list(sq_gate) if sq_gate else [None, None]
                xnt_free = [xnt_gate, xnt_gate]
                chain = [None, None]
                dt_cur = [None, None]
                last_ev = None
                tile_toks = []
                for n in range(ntile):
                    par = n % 2
                    w = [xst_free[par]] + (gates if n < 2 else [])
                    tdma = P.op("SY", ld(xst[par][:], xt_src[n]),
                                waits=w, inc=dsem[par], amt=16)
                    L = sLN[par]
                    t1 = P.op("V", lambda v, L=L, b=xst[par][:]:
                              v.reduce_sum(L[:, 0:1], b, axis=AX.X),
                              waits=[tdma, chain[par]], inc="sV")
                    t2 = P.op("S", lambda s, a=sq[par][:], b=xst[par][:]:
                              s.activation(a, b, AF.Square),
                              waits=[tdma, sq_free[par]], inc="sS")
                    t3 = P.op("V", lambda v, L=L, a=sq[par][:]:
                              v.reduce_sum(L[:, 1:2], a, axis=AX.X),
                              waits=[t2], inc="sV")
                    sq_free[par] = t3
                    t4 = P.op("S", lambda s, L=L: s.activation(
                        L[:, 2:3], L[:, 0:1], AF.Copy, scale=1.0 / E),
                        waits=[t1], inc="sS")
                    t5 = P.op("S", lambda s, L=L: s.activation(
                        L[:, 3:4], L[:, 2:3], AF.Square),
                        waits=[t4], inc="sS")
                    t5b = P.op("S", lambda s, L=L: s.activation(
                        L[:, 5:6], L[:, 3:4], AF.Copy, scale=-1.0, bias=1e-5),
                        waits=[t5], inc="sS")
                    t6 = P.op("S", lambda s, L=L: s.activation(
                        L[:, 4:5], L[:, 1:2], AF.Copy, scale=1.0 / E),
                        waits=[t3], inc="sS")
                    t8 = P.op("S", lambda s, L=L: s.activation(
                        L[:, 6:7], L[:, 4:5], AF.Ln, bias=L[:, 5:6]),
                        waits=[t5b, t6], inc="sS")
                    t9 = P.op("S", lambda s, L=L: s.activation(
                        L[:, 7:8], L[:, 6:7], AF.Exp, scale=-0.5),
                        waits=[t8], inc="sS")
                    ta = P.op("S", lambda s, L=L: s.activation(
                        L[:, 3:4], L[:, 2:3], AF.Copy, scale=-1.0),
                        waits=[t9], inc="sS")
                    tb = P.op("S", lambda s, L=L: s.activation(
                        L[:, 5:6], L[:, 3:4], AF.Copy, scale=L[:, 7:8]),
                        waits=[ta], inc="sS")
                    t11 = P.op("V", lambda v, L=L, a=xnt[par][:], b=xst[par][:]:
                               v.tensor_scalar(a, b, L[:, 7:8], L[:, 5:6],
                                               ALU.mult, ALU.add),
                               waits=[tb, xnt_free[par]], inc="sV")
                    chain[par] = t11
                    xst_free[par] = t11
                    qn, dsT = ("S", "dT0") if par == 0 else ("SY", "dT1")
                    tdt = P.op(qn, lambda e,
                               o=xn_dst[:, :, n * 128:(n + 1) * 128],
                               i_=xnt[par][:]:
                               e.dma_start_transpose(out=o, in_=i_),
                               waits=[t11, dt_gate], inc=dsT, amt=16)
                    xnt_free[par] = tdt
                    dt_cur[par] = tdt
                    last_ev = [t for t in dt_cur if t is not None]
                    tile_toks.append(last_ev)
                    if interleave_cb is not None:
                        interleave_cb(n, tile_toks)
                return last_ev, tile_toks

        with ExitStack() as phb0:
            pb0 = phb0.enter_context
            xstA = [pb0(nc.sbuf_tensor(f"xstA{i}", [128, E], F32))
                    for i in range(2)]
            ph = ExitStack()
            pe = ph.enter_context
            wgc = [pe(nc.sbuf_tensor(f"wgc{i}", [128, E], BF16)) for i in range(2)]
            wuc = [pe(nc.sbuf_tensor(f"wuc{i}", [128, E], BF16)) for i in range(2)]
            wdc = [pe(nc.sbuf_tensor("wdc0", [128, E], BF16))]
            wgT = [pe(nc.sbuf_tensor(f"wgT{i}", [128, EC * 128], BF16))
                   for i in range(2)]
            wuT = [pe(nc.sbuf_tensor(f"wuT{i}", [128, EC * 128], BF16))
                   for i in range(2)]
            sSG = [pe(nc.sbuf_tensor(f"sSG{i}", [128, QW], BF16)) for i in range(2)]
            sGT = xstA

            xnt0 = [BIG0[:, big0c - (i + 1) * E: big0c - i * E] for i in range(2)]
            tok_xnown, xn_tiles = ln_phase(xq_t, OWNC, xn_own, ("dX0", "dX1"),
                                           [tok_consts], "a",
                                           pool_ext=(xstA, xnt0))
            tok_b0T = P.cnt("sT")
            tok_b0V = P.cnt("sV")
            tok_b0S = P.cnt("sS")
            P.flush("b0")

            act_done = [[None] * NQW for _ in range(NHC)]
            wdT_done = [None] * NHC

            with ExitStack() as php1:
                pp = php1.enter_context
                Gps = [pp(nc.psum_tensor(f"Gps{i}", [128, QW], F32)) for i in range(3)]
                Ups = [pp(nc.psum_tensor(f"Ups{i}", [128, QW], F32)) for i in range(3)]

                f_gate = []
                wgc_free = [None, None]
                wuc_free = [None, None]
                wdc_free = [None, None]
                wgT_free = [None, None]
                wuT_free = [None, None]
                gps_free = [None] * 3
                ups_free = [None] * 3
                sg_free = [None, None]
                qctr = [0]

                def transpose_into(src, dst3, waits, par, extra_last=None):
                    ww = list(waits)
                    if extra_last is not None:
                        ww.append(extra_last)
                    tdt = P.op("SY", lambda e, o=dst3, i_=src[:]:
                               e.dma_start_transpose(out=o, in_=i_),
                               waits=ww, inc="dTW0" if par == 0 else "dTW1",
                               amt=16)
                    return [tdt] * (EC // GT)

                for hc in range(NHC):
                    par = hc % 2
                    dsem = "dW0" if par == 0 else "dW1"
                    tg = P.op("G", lambda g, o=wgc[par][:], s=wgr[hc]:
                              g.dma_start(out=o, in_=s),
                              waits=[wgc_free[par]] + (f_gate if hc < 2 else []),
                              inc=dsem, amt=16)
                    tu = P.op("G", lambda g, o=wuc[par][:], s=wur[hc]:
                              g.dma_start(out=o, in_=s),
                              waits=[wuc_free[par]], inc=dsem, amt=16)
                    td = P.op("G", lambda g,
                              o=wdc[0][:].rearrange("p (dc h) -> p dc h", dc=EC),
                              s=wdr[:, :, hc * 128:(hc + 1) * 128]:
                              g.dma_start(out=o, in_=s),
                              waits=[wdc_free[0]], inc=dsem, amt=16)
                    transpose_into(
                        wgc[par],
                        wgT[par][:].rearrange("p (c t) -> p c t", c=EC),
                        [td], par, extra_last=wgT_free[par])
                    transpose_into(
                        wuc[par],
                        wuT[par][:].rearrange("p (c t) -> p c t", c=EC),
                        [td], par, extra_last=wuT_free[par])
                    d_toks = transpose_into(
                        wdc[0],
                        wdT[:, hc, :].rearrange("p (c t) -> p c t", c=EC),
                        [td], par)
                    tall = d_toks[-1]
                    g_toks = [tall] * (EC // GT)
                    u_toks = [tall] * (EC // GT)
                    wgc_free[par] = tall
                    wuc_free[par] = tall
                    wdc_free[0] = tall
                    wdT_done[hc] = tall
                    last_g = None
                    last_u = None
                    for qs in range(NQW):
                        gp = qctr[0] % 3
                        qctr[0] += 1
                        for ec in range(EC):
                            last_g = P.op(
                                "T", lambda t, o=Gps[gp][:],
                                l=wgT[par][:, ec * 128:(ec + 1) * 128],
                                r=xn_own[:, ec, qs * QW:(qs + 1) * QW],
                                st=(ec == 0), sp=(ec == EC - 1):
                                t.matmul(o, l, r, start=st, stop=sp),
                                waits=[g_toks[ec // GT], gps_free[gp]] +
                                      xn_tiles[((qs + 1) * QW) // 128 - 1],
                                inc="sT" if ec == EC - 1 else None)
                        for ec in range(EC):
                            last_u = P.op(
                                "T", lambda t, o=Ups[gp][:],
                                l=wuT[par][:, ec * 128:(ec + 1) * 128],
                                r=xn_own[:, ec, qs * QW:(qs + 1) * QW],
                                st=(ec == 0), sp=(ec == EC - 1):
                                t.matmul(o, l, r, start=st, stop=sp),
                                waits=[u_toks[ec // GT], ups_free[gp]],
                                inc="sT" if ec == EC - 1 else None)
                        tsg = P.op("S", lambda s, o=sSG[qs % 2][:], i_=Gps[gp][:]:
                                   s.activation(o, i_, AF.Silu, scale=1.0 / 127.0),
                                   waits=[last_g, sg_free[qs % 2]], inc="sS")
                        gps_free[gp] = tsg
                        sp_ = qs % 2
                        tm = P.op("V", lambda v, o=ACT[:, hc, qs * QW:(qs + 1) * QW],
                                  a=sSG[sp_][:], b=Ups[gp][:]:
                                  v.tensor_tensor(o, a, b, ALU.mult),
                                  waits=[tsg, last_u] +
                                        ([tok_b0T] if hc == NHC - 1 else []),
                                  inc="sV")
                        ups_free[gp] = tm
                        sg_free[qs % 2] = tm
                        act_done[hc][qs] = tm
                    wgT_free[par] = last_g
                    wuT_free[par] = last_u

                tok_f1T = P.cnt("sT")
                tok_f1V = P.cnt("sV")
                tok_f1S = P.cnt("sS")
                P.flush("f1")

            with ExitStack() as php2:
                pp = php2.enter_context
                Dps = [pp(nc.psum_tensor(f"Dps{i}", [128, E], F32)) for i in range(4)]
                dps_free = [None] * 4
                gt_free = [tok_b0V, tok_b0V]
                ng_store = 0
                for q2 in range(OWNC):
                    dp = q2 % 4
                    qs = (q2 * 128) // QW
                    last = None
                    for ew in range(NEW):
                        for hc in range(NHC):
                            w = [act_done[hc][qs], wdT_done[hc], dps_free[dp]]
                            if q2 == 0 and ew == 0 and hc == 0:
                                w += [tok_f1V, tok_f1S]
                            last = P.op(
                                "T", lambda t, o=Dps[dp][:, ew * EW:(ew + 1) * EW],
                                l=ACT[:, hc, q2 * 128:(q2 + 1) * 128],
                                r=wdT[:, hc, ew * EW:(ew + 1) * EW],
                                st=(hc == 0), sp=(hc == NHC - 1):
                                t.matmul(o, l, r, start=st, stop=sp),
                                waits=w, inc="sT" if hc == NHC - 1 else None)
                    gp = q2 % 2
                    tsil = P.op("S", lambda s, o=sGT[gp][:], i_=Dps[dp][:]:
                                s.activation(o, i_, AF.Silu, scale=C2),
                                waits=[last, gt_free[gp]], inc="sS")
                    dps_free[dp] = tsil
                    tst = P.op("G", lambda g, o=gsp_t[q2], i_=sGT[gp][:]:
                               g.dma_start(out=o, in_=i_),
                               waits=[tsil], inc="dG0" if gp == 0 else "dG1",
                               amt=16)
                    gt_free[gp] = tst
                    ng_store += 1
                tok_f2T = P.cnt("sT")
                tok_f2S = P.cnt("sS")
                P.flush("f2")
            ph.close()

            with ExitStack() as phc:
                pc = phc.enter_context
                sPK = pc(nc.sbuf_tensor("sPK", [128, S], BF16))
                sTRI = pc(nc.sbuf_tensor("sTRI", [128, DGC * QBS], BF16))
                sPQ = Kp[1][:, 2 * E:2 * E + OWN]
                gdone = [("dG0", 16 * ((ng_store + 1) // 2)),
                         ("dG1", 16 * (ng_store // 2))]
                cgate = [tok_f1T, tok_f1V, tok_f2T, tok_f2S] + gdone
                P.op("G", lambda g: g.dma_start(out=sPQ, in_=d["posq"]),
                     waits=[tok_f2T], inc="dP", amt=16)
                tpq = ("dP", 16)
                P.op("G", lambda g: g.dma_start(out=sPK[:], in_=d["posk"]),
                     waits=cgate, inc="dK", amt=16)
                tpk = ("dK", 16)


                tok_vaug = [None, None]
                for i in range(2):
                    tok_vaug[i] = P.op("V", lambda v, i=i: v.memset(
                        Vaug[i][:, :, :, 64:65], 1.0),
                        waits=[tok_f2T, tok_f2S], inc="sV")

                SC2W = max(2 * QBS, 2 * GT * 128, 2 * QW)
                SC2 = [pc(nc.psum_tensor(f"SC2_{i}", [128, SC2W], F32))
                       for i in range(2)]
                AVPD = max(QBS, 512)
                KVPD = max(KW, 512)
                AV2 = pc(nc.psum_tensor("AV2", [128, max(2 * AVPD, E)], F32))
                KV2 = pc(nc.psum_tensor("KV2", [128, max(2 * KVPD, E)], F32))
                AVps = [AV2[0:65, 0:QBS], AV2[0:65, AVPD:AVPD + QBS]]
                KVp = [KV2[:, 0:KW], KV2[:, KVPD:KVPD + KW]]
                Qps = [SC2[0][:, 0:QW], AV2[:, 0:QW], AV2[:, QW:2 * QW]]
                SQp1 = [SC2[1][0:1, 0:QW], SC2[1][0:1, QW:2 * QW]]
                wq_free = [None, None]
                qps_free = [None, None, None]
                sqst = dict(free=[tok_f2T, tok_f2T])
                b4st = dict(sqmm_free=[tok_f2S, tok_f2S], qctr=0)
                tok_mq = [None] * NP
                qbs_per_qs = max(1, 4 // NQW)

                qst = [dict(), dict()]

                def q_dma(p):
                    par = p % 2
                    dsem = "dS0" if par == 0 else "dS1"
                    qst[par]["tw"] = P.op(
                        "G", lambda g, o=Wsl[par][:],
                        s=wqr[:, :, p * 128:(p + 1) * 128]:
                        g.dma_start(out=o, in_=s),
                        waits=[wq_free[par], tok_f2T], inc=dsem, amt=16)

                def q_qs(p, qs):
                    par = p % 2
                    qd = qst[par]
                    qp = b4st["qctr"] % 3
                    b4st["qctr"] += 1
                    last = None
                    for ec in range(EC):
                        w = [qd["tw"], qps_free[qp]] + tok_xnown
                        if p == 0 and ec == 0:
                            w += [tok_f2S, tok_f1V]
                        last = P.op(
                            "T", lambda t, o=Qps[qp][:],
                            l=Wsl[par][:, ec, :],
                            r=xn_own[:, ec, qs * QW:(qs + 1) * QW],
                            st=(ec == 0), sp=(ec == EC - 1):
                            t.matmul(o, l, r, start=st, stop=sp),
                            waits=w, inc="sT" if ec == EC - 1 else None)
                    wq_free[par] = last
                    tev = P.op("V", lambda v,
                               o=Q_all[:, p, qs * QW:(qs + 1) * QW],
                               a=Qps[qp][:],
                               b=sPQ[:, qs * QW:(qs + 1) * QW]:
                               v.tensor_tensor(o, a, b, ALU.add),
                               waits=[last, tpq], inc="sV")
                    qps_free[qp] = tev
                    qd[("tev", qs)] = tev

                def q_sq(p, qs):
                    par = p % 2
                    qd = qst[par]
                    sqi = qs % 2
                    tsq = P.op("V", lambda v, o=sSQ[sqi][:, 0:QW],
                               a=Q_all[:, p, qs * QW:(qs + 1) * QW]:
                               v.tensor_tensor(o, a, a, ALU.mult),
                               waits=[qd[("tev", qs)], sqst["free"][sqi]],
                               inc="sV")
                    tmm = P.op("T", lambda t, o=SQp1[sqi][0:1, 0:QW],
                               l=sONE[:], r=sSQ[sqi][:, 0:QW]:
                               t.matmul(o, l, r, start=True, stop=True),
                               waits=[tsq, tok_consts, tok_f1V,
                                      b4st["sqmm_free"][sqi]], inc="sT")
                    sqst["free"][sqi] = tmm
                    rlast = None
                    for jj in range(qbs_per_qs):
                        j = qs * qbs_per_qs + jj
                        rlast = P.op(
                            "V", lambda v, p=p, j=j, jj=jj, sqi=sqi:
                            v.reduce_max(
                                sMQ[0:1, p * 4 + j:p * 4 + j + 1],
                                SQp1[sqi][0:1, jj * QBS:(jj + 1) * QBS],
                                axis=AX.X),
                            waits=[tmm, rlast], inc="sV")
                    b4st["sqmm_free"][sqi] = rlast
                    if qs == NQW - 1:
                        tok_mq[p] = rlast

                def qpair_units(p):
                    units = []

                    def u0(p=p):
                        q_dma(p)
                        q_qs(p, 0)
                    units.append((3 * p, u0))
                    for k in range(1, NQW):
                        def uk(p=p, k=k):
                            q_qs(p, k)
                            if k >= 2:
                                q_sq(p, k - 2)
                        units.append((3 * p + (k + 1) // 2, uk))

                    def ulast(p=p):
                        for k in range(max(0, NQW - 2), NQW):
                            q_sq(p, k)
                    units.append((3 * p + 2, ulast))
                    return units

                st = dict(kread=[tok_f2T, tok_f2T],
                          vread=[tok_vaug[0], tok_vaug[1]],
                          bm_free=[None, None],
                          sc2=[None, None],
                          kvb=[tok_f2S, tok_f2S],
                          av_free=[None, None],
                          expb_free=[tok_f2T] * 2,
                          sl_free=[None] * 4, bm_tok=[None, None],
                          k_tok=[None, None], v_tok=[None, None],
                          rcp_free=[None] * 4, rcf_free=[None, None],
                          rs_free=tok_f2T)
                kvst = [dict(), dict()]
                TPK = KW // 128

                def kv_units(p, ks, vs, xw):
                    par = p % 2
                    dsem = "dV0" if par == 0 else "dV1"
                    kv = kvst[par]
                    units = []

                    def u_dma():
                        kv["rm_tok2"] = None
                        kv["twk"] = P.op("G", lambda g, o=Wsl[ks][:],
                                         s=wkr[:, :, p * 128:(p + 1) * 128]:
                                         g.dma_start(out=o, in_=s),
                                         waits=[st["sl_free"][ks], tok_f2T],
                                         inc=dsem, amt=16)
                        kv["twv"] = P.op("G", lambda g, o=Wsl[vs][:],
                                         s=wvr[:, :, p * 128:(p + 1) * 128]:
                                         g.dma_start(out=o, in_=s),
                                         waits=[st["sl_free"][vs], tok_f2T],
                                         inc=dsem, amt=16)
                    units.append((0, u_dma))

                    def emit_kg(kg):
                        bi = kg % 2
                        bank = KVp[bi]
                        last = None
                        xt = xw((kg + 1) * TPK - 1)
                        for ec in range(EC):
                            w = [kv["twv"], st["kvb"][bi]] + xt
                            if ec == 0:
                                w.append(tok_f1V)
                            last = P.op(
                                "T", lambda t, o=bank[:, 0:KW],
                                l=Wsl[ks][:, ec, :],
                                r=xn_kv[:, ec, kg * KW:(kg + 1) * KW],
                                st_=(ec == 0), sp=(ec == EC - 1):
                                t.matmul(o, l, r, start=st_, stop=sp),
                                waits=w,
                                inc="sT" if ec == EC - 1 else None)
                        tev = P.op("V", lambda v,
                                   o=Kp[par][:, kg * KW:(kg + 1) * KW],
                                   a=bank[:, 0:KW],
                                   b=sPK[:, kg * KW:(kg + 1) * KW]:
                                   v.tensor_tensor(o, a, b, ALU.add),
                                   waits=[last, tpk, st["kread"][par]],
                                   inc="sV")
                        st["kvb"][bi] = tev
                        kv["lastk"] = last
                        kv["ktok"] = tev

                    for kg_ in range(NKG):
                        units.append(((kg_ + 1) * TPK - 1,
                                      lambda kg_=kg_: emit_kg(kg_)))

                    def u_setk():
                        st["sl_free"][ks] = kv["lastk"]
                        st["k_tok"][par] = kv["ktok"]
                    units.append((NKG * TPK - 1, u_setk))

                    def u_norm1(kg):
                        sqi = kg % 2
                        tsq = P.op("V", lambda v, sqi=sqi,
                                   o=sSQ[sqi][:, 0:KW],
                                   a=Kp[par][:, kg * KW:(kg + 1) * KW]:
                                   v.tensor_tensor(o, a, a, ALU.mult),
                                   waits=[st["kvb"][kg % 2],
                                          sqst["free"][sqi], kv["ktok"]],
                                   inc="sV")
                        bi = kg % 2
                        tmm = P.op("T", lambda t, bi=bi, sqi=sqi,
                                   o=KVp[bi][0:1, 0:KW]:
                                   t.matmul(o, sONE[:], sSQ[sqi][:, 0:KW],
                                            start=True, stop=True),
                                   waits=[tsq], inc="sT")
                        sqst["free"][sqi] = tmm
                        col = 1 if kg == 0 else 2
                        trd = P.op("V", lambda v, col=col, bi=bi:
                                   v.reduce_max(sMK[0:1, col:col + 1],
                                                KVp[bi][0:1, 0:KW],
                                                axis=AX.X),
                                   waits=[tmm, kv.get("rm_tok2")], inc="sV")
                        st["kvb"][bi] = trd
                        if kg == 0:
                            kv["rm_tok2"] = P.op(
                                "V", lambda v: v.tensor_copy(
                                    sMK[0:1, 0:1], sMK[0:1, 1:2]),
                                waits=[trd], inc="sV")
                        else:
                            kv["rm_tok2"] = P.op(
                                "V", lambda v: v.tensor_tensor(
                                    sMK[0:1, 0:1], sMK[0:1, 0:1],
                                    sMK[0:1, 2:3], ALU.max),
                                waits=[trd, kv["rm_tok2"]], inc="sV")
                        kv["rm_tok"] = kv["rm_tok2"]

                    for kg_ in range(NKG):
                        units.append(((kg_ + 1) * TPK - 1,
                                      lambda kg_=kg_: u_norm1(kg_)))

                    def emit_vc(c):
                        bi = c % 2
                        bank = KVp[bi]
                        last = None
                        xt = xw(c)
                        for ec in range(EC):
                            w = [kv["twv"], st["kvb"][bi]] + xt
                            if ec == 0:
                                w.append(tok_f1V)
                            last = P.op(
                                "T", lambda t, o=bank[:, 0:128],
                                l=xn_kv[:, ec, c * 128:(c + 1) * 128],
                                r=Wsl[vs][:, ec, :],
                                st_=(ec == 0), sp=(ec == EC - 1):
                                t.matmul(o, l, r, start=st_, stop=sp),
                                waits=w,
                                inc="sT" if ec == EC - 1 else None)
                        tev = P.op("S", lambda s_,
                                   o=Vaug[par][:, c, :, 0:64],
                                   i_=bank[:, 0:128].rearrange(
                                       "p (two sv) -> p two sv", two=2):
                                   s_.copy(o, i_),
                                   waits=[last, st["vread"][par]],
                                   inc="sS")
                        st["kvb"][bi] = tev
                        if c == KVC - 1:
                            st["sl_free"][vs] = last
                        kv["lastv"] = tev

                    for c_ in range(KVC):
                        units.append((c_, lambda c_=c_: emit_vc(c_)))

                    def u_bound():
                        st["v_tok"][par] = kv["lastv"]
                        t1 = P.op("V", lambda v, p=p: v.tensor_scalar(
                            sMK[0:1, 4:8], sMQ[0:1, p * 4:p * 4 + 4],
                            sMK[0:1, 0:1], 0.5, ALU.add, ALU.mult),
                            waits=[kv["rm_tok"], tok_mq[p]], inc="sV")
                        t2 = P.op("V", lambda v: v.tensor_scalar(
                            sMK[0:1, 4:8], sMK[0:1, 4:8], SCLAMP, -cfg.SM,
                            ALU.min, ALU.mult), waits=[t1], inc="sV")
                        t2b = P.op("V", lambda v: v.tensor_scalar(
                            sMK[0:1, 4:8], sMK[0:1, 4:8], 12.0, None,
                            ALU.add), waits=[t2], inc="sV")
                        t3 = P.op("V", lambda v: v.tensor_copy(
                            sBSB[:], sMK[0:1, 4:8]), waits=[t2b], inc="sV")
                        lastb = None
                        for j in range(4):
                            lastb = P.op(
                                "T", lambda t, j=j:
                                t.matmul(KVp[0][:, j:j + 1], sOR[0:1, :],
                                         sBSB[0:1, j:j + 1],
                                         start=True, stop=True),
                                waits=[t3, tok_consts, st["kvb"][0]],
                                inc="sT" if j == 3 else None)
                        bm_tok = None
                        for j in range(4):
                            bm_tok = P.op(
                                "V", lambda v, j=j, par=par:
                                v.tensor_scalar(
                                    sBM[par][:, j * KVC:(j + 1) * KVC],
                                    sWL[:, j * KVC:(j + 1) * KVC],
                                    KVp[0][:, j:j + 1], None, ALU.add),
                                waits=[lastb, tok_consts,
                                       st["bm_free"][par]],
                                inc="sV")
                        st["bm_tok"][par] = bm_tok
                        st["kvb"][0] = bm_tok
                    units.append((KVC - 1, u_bound))
                    return units

                ntile_kv = S // 128
                b34q = []
                for p_ in range(NP):
                    b34q += qpair_units(p_)
                holder = {}
                b34q += kv_units(0, 2, 3, lambda r: holder["toks"][r])
                b34q.sort(key=lambda it: it[0])
                qpos = [0]

                def b34_cb(n, toks):
                    holder["toks"] = toks
                    c = 0
                    while (qpos[0] < len(b34q) and b34q[qpos[0]][0] <= n
                           and c < 3):
                        b34q[qpos[0]][1]()
                        qpos[0] += 1
                        c += 1

                tok_xnkv, _ = ln_phase(
                    xkv_t, ntile_kv, xn_kv, ("dX0", "dX1"),
                    [tok_f1T, tok_f1V], "b",
                    xnt_ext=[Kp[1][:, i * E:(i + 1) * E]
                             for i in range(2)],
                    xnt_gate=tok_f2T,
                    dt_gate=tok_f2T, sq_gate=gdone,
                    sq_ext=[xstA[0], xstA[1]],
                    interleave_cb=b34_cb)
                while qpos[0] < len(b34q):
                    b34q[qpos[0]][1]()
                    qpos[0] += 1
                tok_b4T = P.cnt("sT")
                tok_b4V = P.cnt("sV")
                tok_b4S = P.cnt("sS")
                P.flush("b34")

                tok_Q = tok_b4V
                st["sc2"] = [tok_Q, tok_b4S]
                st["av_free"] = [tok_Q, tok_Q]
                st["sl_free"][0] = tok_b4T
                st["sl_free"][1] = tok_b4T

                with ExitStack() as ph5:
                    p5 = ph5.enter_context
                    sRCP = [p5(nc.sbuf_tensor(f"sRCP{i}", [1, QBS], BF16))
                            for i in range(4)]
                    sRCF = [p5(nc.sbuf_tensor(f"sRCF{i}", [1, QBS], F32))
                            for i in range(2)]
                    ttri = P.op("SY", ld(sTRI[:], d["tri"]),
                                waits=[tok_f1T, tok_f1V], inc="dC", amt=16)
                    fillers = []
                    pend = [None]

                    def attention(p):
                        par = p % 2
                        for j in range(4):
                            ext = cfg.EXT[j]
                            avdep = [None, None]
                            ev_tok = {}

                            def emit_sc(c):
                                bi = c % 2
                                lm = None
                                for h in range(2):
                                    lm = P.op(
                                        "T", lambda t, bi=bi, h=h, c=c, j=j, par=par,
                                        p=p:
                                        t.matmul(
                                            SC2[bi][:, h * QBS:(h + 1) * QBS],
                                            Kp[par][h * 64:(h + 1) * 64,
                                                    c * 128:(c + 1) * 128],
                                            Q_all[h * 64:(h + 1) * 64, p,
                                                  j * QBS:(j + 1) * QBS],
                                            start=True, stop=True),
                                        waits=[st["k_tok"][par], tok_Q,
                                               st["sc2"][bi]],
                                        inc="sT" if h == 1 else None)
                                st["kread"][par] = lm
                                col = j * KVC + c
                                te = P.op(
                                    "S", lambda s, bi=bi, col=col, par=par:
                                    s.activation(expb[bi],
                                                 SC2[bi][:, 0:2 * QBS], AF.Exp,
                                                 scale=cfg.SM,
                                                 bias=sBM[par][:, col:col + 1]),
                                    waits=[lm, st["bm_tok"][par],
                                           st["expb_free"][bi]],
                                    inc="sS")
                                st["sc2"][bi] = te
                                ev = te
                                if c >= ext - DGC:
                                    r = c - (ext - DGC)
                                    for h in range(2):
                                        ev = P.op(
                                            "V", lambda v, bi=bi, r=r, h=h:
                                            v.tensor_tensor(
                                                expb[bi][:, h * QBS:(h + 1) * QBS],
                                                expb[bi][:, h * QBS:(h + 1) * QBS],
                                                sTRI[:, r * QBS:(r + 1) * QBS],
                                                ALU.mult),
                                            waits=[te, ttri], inc="sV")
                                ev_tok[c] = ev

                            def emit_av(c):
                                bi = c % 2
                                lm = None
                                for h in range(2):
                                    w = [ev_tok.pop(c) if h == 1 else ev_tok[c],
                                         st["v_tok"][par]]
                                    if c == 0:
                                        w.append(st["av_free"][h])
                                    lm = P.op(
                                        "T", lambda t, h=h, bi=bi, c=c, par=par,
                                        st_=(c == 0), sp=(c == ext - 1):
                                        t.matmul(AVps[h][:], Vaug[par][:, c, h, :],
                                                 expb[bi][:, h * QBS:(h + 1) * QBS],
                                                 start=st_, stop=sp),
                                        waits=w, inc="sT" if h == 1 else None)
                                if c == ext - 1:
                                    avdep[0] = lm
                                    avdep[1] = lm
                                st["vread"][par] = lm
                                st["expb_free"][bi] = lm

                            for c in range(ext):
                                if c == 2 and pend[0] is not None:
                                    pend[0]()
                                    pend[0] = None
                                if c >= 2:
                                    emit_av(c - 2)
                                emit_sc(c)
                                if c % 2 == 1 and fillers:
                                    fillers.pop(0)()
                            if pend[0] is not None:
                                pend[0]()
                                pend[0] = None
                            for c in range(max(0, ext - 2), ext):
                                emit_av(c)
                            if fillers:
                                fillers.pop(0)()

                            rp = 2 * (j % 2)
                            tl2s = [None, None]
                            for h in range(2):
                                tl1 = P.op("S", lambda s, h=h:
                                           s.activation(sRCF[h][0:1, :],
                                                        AVps[h][64:65, :],
                                                        AF.Ln),
                                           waits=[avdep[h],
                                                  st["rcf_free"][h],
                                                  st["rcp_free"][rp + h],
                                                  tok_f1V, tok_f2S] + gdone,
                                           inc="sS")
                                tl2s[h] = P.op(
                                    "S", lambda s, h=h, rp=rp:
                                    s.activation(sRCP[rp + h][0:1, :],
                                                 sRCF[h][0:1, :],
                                                 AF.Exp, scale=-1.0),
                                    waits=[tl1], inc="sS")
                                st["rcf_free"][h] = tl2s[h]

                            def fin(j=j, p=p, par=par, avdep=avdep, tl2s=tl2s,
                                    rp=rp):
                                tcp = [None, None]
                                tbc = [None, None]
                                for h in range(2):
                                    tbc[h] = P.op(
                                        "T", lambda t, h=h, rp=rp:
                                        t.matmul(KVp[h][0:64, 0:QBS],
                                                 sOR[0:1, 0:64],
                                                 sRCP[rp + h][0:1, :],
                                                 start=True, stop=True),
                                        waits=[tl2s[h], tok_consts,
                                               st["kvb"][h]],
                                        inc="sT" if h == 1 else None)
                                    st["rcp_free"][rp + h] = tbc[h]
                                for h in range(2):
                                    tcp[h] = P.op(
                                        "V", lambda v, h=h:
                                        v.tensor_copy(sRS[h * 64:(h + 1) * 64, :],
                                                      KVp[h][0:64, 0:QBS]),
                                        waits=[tbc[1], st["rs_free"]], inc="sV")
                                st["kvb"][0] = tcp[1]
                                st["kvb"][1] = tcp[1]
                                for h in range(2):
                                    tdv = P.op(
                                        "V", lambda v, h=h, p=p, j=j:
                                        v.tensor_tensor(
                                            AO[h * 64:(h + 1) * 64, p,
                                               j * QBS:(j + 1) * QBS],
                                            AVps[h][0:64, :],
                                            sRS[h * 64:(h + 1) * 64, :],
                                            ALU.mult),
                                        waits=[tcp[1], avdep[h], tok_b4T],
                                        inc="sV")
                                    st["av_free"][h] = tdv
                                    st["rs_free"] = tdv
                            pend[0] = fin
                        st["bm_free"][par] = P.cnt("sS")
                        while fillers:
                            fillers.pop(0)()

                    for p in range(NP):
                        if p + 1 < NP:
                            fillers.extend(
                                fn for _, fn in kv_units(
                                    p + 1, (p + 1) % 2, 2 + (p + 1) % 2,
                                    lambda r: tok_xnkv))
                        attention(p)
                    if pend[0] is not None:
                        pend[0]()
                        pend[0] = None
                    two = P.op("G", lambda g: g.dma_start(
                        out=BIG0[:, 0:EC * E].rearrange("p (rc e) -> p rc e", rc=EC),
                        in_=wor),
                               waits=[st["k_tok"][(NP - 1) % 2],
                                      st["v_tok"][(NP - 1) % 2]],
                               inc="dS0", amt=16)
                    tok_b5T = P.cnt("sT")
                    tok_b5V = P.cnt("sV")
                    tok_b5S = P.cnt("sS")
                    P.flush("b5")

            if True:
                with ExitStack() as ph6:
                    p6 = ph6.enter_context
                    OPs = [p6(nc.psum_tensor(f"OPs{i}", [128, E], F32))
                           for i in range(4)]
                    gt6 = [p6(nc.sbuf_tensor(f"gt6{i}", [128, E], F32))
                           for i in range(2)]
                    xt6 = xstA
                    ot6 = [p6(nc.sbuf_tensor(f"ot6{i}", [128, E], F32))
                           for i in range(2)]

                    gates6 = [tok_b5T, tok_b5V, tok_b5S,
                              ("dG0", 16 * ((ng_store + 1) // 2)),
                              ("dG1", 16 * (ng_store // 2))]
                    ops_free = [None] * 4
                    g_free = [None, None]
                    x_free = [None, None]
                    o_free = [None, None]
                    t_free = None
                    nout = 0
                    for q2 in range(OWNC):
                        par = q2 % 2
                        dsem = "dR0" if par == 0 else "dR1"
                        tgl = P.op("G", lambda g, o=gt6[par][:], s=gsp_t[q2]:
                                   g.dma_start(out=o, in_=s),
                                   waits=[g_free[par]] + (gates6 if q2 < 2 else []),
                                   inc="dR2" if par == 0 else "dR3", amt=16)
                        txl = P.op("SY", lambda sy, o=xt6[par][:], s=xq_t[q2]:
                                   sy.dma_start(out=o, in_=s),
                                   waits=[x_free[par]] +
                                         (gates6 if q2 < 2 else []),
                                   inc=dsem, amt=16)
                        op = q2 % 4
                        last = None
                        for ew in range(NEW):
                            for rc in range(NP):
                                w = [ops_free[op]]
                                if q2 == 0 and ew == 0 and rc == 0:
                                    w += [tok_b5V, tok_b5S]
                                w += [tok_b5V, two]
                                last = P.op(
                                    "T", lambda t, op=op, ew=ew, rc=rc, q2=q2,
                                    st_=(rc == 0), sp=(rc == NP - 1):
                                    t.matmul(OPs[op][:, ew * EW:(ew + 1) * EW],
                                             AO[:, rc, q2 * 128:(q2 + 1) * 128],
                                             Wo_sb[:, rc, ew * EW:(ew + 1) * EW],
                                             start=st_, stop=sp),
                                    waits=w, inc="sT" if rc == NP - 1 else None)
                        lev = None
                        for ew in range(NEW):
                            tmul = P.op("V", lambda v, op=op, ew=ew, par=par:
                                        v.tensor_tensor(
                                            ot6[par][:, ew * EW:(ew + 1) * EW],
                                            OPs[op][:, ew * EW:(ew + 1) * EW],
                                            gt6[par][:, ew * EW:(ew + 1) * EW],
                                            ALU.mult),
                                        waits=[last, txl, tgl, t_free,
                                               o_free[par]],
                                        inc="sV")
                            lev = P.op("V", lambda v, op=op, ew=ew, par=par:
                                       v.tensor_tensor(
                                           ot6[par][:, ew * EW:(ew + 1) * EW],
                                           ot6[par][:, ew * EW:(ew + 1) * EW],
                                           xt6[par][:, ew * EW:(ew + 1) * EW],
                                           ALU.add),
                                       waits=[tmul], inc="sV")
                            t_free = lev
                        ops_free[op] = lev
                        g_free[par] = lev
                        x_free[par] = lev
                        tst = P.op("SY", lambda sy, o=out_t[q2], i_=ot6[par][:]:
                                   sy.dma_start(out=o, in_=i_),
                                   waits=[lev], inc="dO0" if par == 0 else "dO1",
                                   amt=16)
                        o_free[par] = tst
                        nout += 1
                    P.op("SY", lambda sy: sy.nop(),
                         waits=[("dO0", 16 * ((nout + 1) // 2)),
                                ("dO1", 16 * (nout // 2))])
                    P.flush("b6")

    return nc



def kernel(L0_attn_x, L0_attn_Wq, L0_attn_Wk, L0_attn_Wv, L0_attn_Wo,
           L0_ffn_w_gate, L0_ffn_w_up, L0_ffn_w_down):
    cfg = FULL
    x = np.asarray(L0_attn_x, dtype=np.float32)
    maps, owns = host_inputs(
        cfg, x,
        np.asarray(L0_attn_Wq, np.float32), np.asarray(L0_attn_Wk, np.float32),
        np.asarray(L0_attn_Wv, np.float32), np.asarray(L0_attn_Wo, np.float32),
        np.asarray(L0_ffn_w_gate, np.int32), np.asarray(L0_ffn_w_up, np.int32),
        np.asarray(L0_ffn_w_down, np.int32))
    nc = build_program(cfg)
    res = run_bass_kernel_spmd(nc, maps, core_ids=list(range(cfg.ncores)))
    out = np.empty((cfg.NB, cfg.S, cfg.E), np.float32)
    for c, (b, own) in enumerate(owns):
        out[b][own] = np.asarray(res.results[c]["out"], np.float32)
    return out


# version 25
# speedup vs baseline: 1.0338x; 1.0122x over previous
import math
import numpy as np
import ml_dtypes
from contextlib import ExitStack

import concourse.bass as bass
import concourse.mybir as mybir
from concourse.bass_utils import run_bass_kernel_spmd

F32 = mybir.dt.float32
BF16 = mybir.dt.bfloat16
I32 = mybir.dt.int32
AF = mybir.ActivationFunctionType
ALU = mybir.AluOpType
AX = mybir.AxisListType

NEG = -1.0e30
SCLAMP = 1280.0


class Cfg:
    def __init__(self, S=4096, E=1024, NH=16, HID=3072, NB=4):
        self.S, self.E, self.NH, self.HID, self.NB = S, E, NH, HID, NB
        self.DH = 64
        self.NP = NH // 2
        self.EC = E // 128
        self.KVC = S // 128
        self.QBS = S // 8
        self.DGC = self.QBS // 128
        self.OWN = S // 2
        self.OWNC = self.OWN // 128
        self.NHC = HID // 128
        self.EXT = [(j + 1) * 2 * self.DGC for j in range(4)]
        self.QW = min(512, self.OWN)
        self.NQW = self.OWN // self.QW
        self.KW = min(512, S, 2 * self.QBS)
        self.NKG = S // self.KW
        self.EW = min(512, E)
        self.NEW = E // self.EW
        self.GT = min(4, self.EC)
        self.SM = 1.0 / math.sqrt(E)
        self.ncores = 2 * NB


FULL = Cfg()



def core_layout(cfg, t):
    D = cfg.DGC
    own = np.concatenate([np.arange(cfg.QBS) + cfg.QBS * (2 * j + t)
                          for j in range(4)])
    perm = np.arange(cfg.KVC)
    if t == 0:
        perm = perm.reshape(-1, 2, D)[:, ::-1, :].reshape(-1)
    kvrows = (perm[:, None] * 128 + np.arange(128)[None, :]).reshape(-1)
    return own, perm, kvrows


def wl_table(cfg, t):
    D = cfg.DGC
    wl = np.zeros((4, cfg.KVC), np.float32)
    for j in range(4):
        ext = cfg.EXT[j]
        for c in range(cfg.KVC):
            if c >= ext:
                wl[j, c] = NEG
            elif c >= ext - D:
                wl[j, c] = 0.0
            elif t == 0 and 2 * D * j <= c < 2 * D * j + D:
                wl[j, c] = NEG
    return wl


def pos_table(cfg):
    EPS = float(np.finfo(np.float32).eps)
    idx = np.arange(cfg.S, dtype=np.float64)
    scale = 1.0 / (10000.0 ** (np.arange(0, 64, 2, dtype=np.float64) / 64.0) + EPS)
    ang = idx[:, None] * scale[None, :]
    pos = np.stack([np.sin(ang), np.cos(ang)], axis=-1).reshape(cfg.S, 64)
    return pos.astype(np.float32)


def host_inputs(cfg, x, Wq, Wk, Wv, Wo, wg, wu, wd):
    bf = ml_dtypes.bfloat16
    pos = pos_table(cfg)
    ident = np.eye(128, dtype=bf)
    tri = np.zeros((128, cfg.DGC * cfg.QBS), np.float32)
    for r in range(cfg.DGC):
        k = 128 * r + np.arange(128)[:, None]
        q = np.arange(cfg.QBS)[None, :]
        tri[:, r * cfg.QBS:(r + 1) * cfg.QBS] = (k <= q)
    tri = tri.astype(bf)
    onescol = np.ones((128, 1), dtype=bf)
    onesrow = np.ones((1, 128), dtype=bf)
    cvec = np.zeros((128, 4), np.float32)
    cvec[:, 0] = 1e-5

    shared = dict(
        ident=ident, tri=tri, onescol=onescol, onesrow=onesrow, cvec=cvec,
        Wq=np.ascontiguousarray(Wq), Wk=np.ascontiguousarray(Wk),
        Wv=np.ascontiguousarray(Wv), Wo=np.ascontiguousarray(Wo),
        wg=np.ascontiguousarray(wg), wu=np.ascontiguousarray(wu),
        wd=np.ascontiguousarray(wd),
    )
    maps, owns = [], []
    for c in range(cfg.ncores):
        b, t = c // 2, c % 2
        own, perm, kvrows = core_layout(cfg, t)
        wl = wl_table(cfg, t)
        WL = np.repeat(wl.reshape(1, -1), 128, axis=0).astype(np.float32)
        pq = pos[own].T
        pk = pos[kvrows].T
        m = dict(shared)
        m["x_q"] = np.ascontiguousarray(x[b][own])
        m["x_kv"] = np.ascontiguousarray(x[b][kvrows])
        m["posq"] = np.ascontiguousarray(np.concatenate([pq, pq], 0).astype(np.float32))
        m["posk"] = np.ascontiguousarray(np.concatenate([pk, pk], 0).astype(np.float32))
        m["WL"] = WL
        maps.append(m)
        owns.append((b, own))
    return maps, owns



class PB:
    ENG = ("SY", "G", "T", "V", "S")

    def __init__(self, nc):
        self.nc = nc
        self.q = {k: [] for k in self.ENG}
        self.sems = {}
        self.hw = {k: {} for k in self.ENG}

    def mksem(self, stack, *names):
        for n in names:
            self.sems[n] = [stack.enter_context(self.nc.semaphore(n)), 0]

    def cnt(self, name):
        return (name, self.sems[name][1])

    def op(self, qn, fn, waits=(), inc=None, amt=1):
        w = [x for x in waits if x is not None]
        self.q[qn].append((fn, w, inc, amt))
        if inc is not None:
            self.sems[inc][1] += amt
            return (inc, self.sems[inc][1])
        return None

    def flush(self, name):
        def runner(qn, ops):
            def body(eng):
                hw = self.hw[qn]
                for fn, waits, inc, amt in ops:
                    for (sn, val) in waits:
                        if val > hw.get(sn, 0):
                            eng.wait_ge(self.sems[sn][0], val)
                            hw[sn] = val
                    ins = fn(eng)
                    if inc is not None:
                        ins.then_inc(self.sems[inc][0], amt)
            return body

        with self.nc.Block(name) as block:
            tgt = dict(SY=block.sync, G=block.gpsimd, T=block.tensor,
                       V=block.vector, S=block.scalar)
            for qn in self.ENG:
                if self.q[qn]:
                    tgt[qn](runner(qn, self.q[qn]))
                    self.q[qn] = []


def build_program(cfg):
    nc = bass.Bass("TRN2", target_bir_lowering=False)
    S, E, NP, EC, KVC = cfg.S, cfg.E, cfg.NP, cfg.EC, cfg.KVC
    QBS, DGC, OWN, OWNC = cfg.QBS, cfg.DGC, cfg.OWN, cfg.OWNC
    NHC, QW, NQW, KW, NKG = cfg.NHC, cfg.QW, cfg.NQW, cfg.KW, cfg.NKG
    GT, EW, NEW = cfg.GT, cfg.EW, cfg.NEW
    C2 = 1.0 / (127.0 * 127.0)

    d = {}
    d["x_q"] = nc.dram_tensor("x_q", [OWN, E], F32, kind="ExternalInput").ap()
    d["x_kv"] = nc.dram_tensor("x_kv", [S, E], F32, kind="ExternalInput").ap()
    d["posq"] = nc.dram_tensor("posq", [128, OWN], F32, kind="ExternalInput").ap()
    d["posk"] = nc.dram_tensor("posk", [128, S], F32, kind="ExternalInput").ap()
    d["WL"] = nc.dram_tensor("WL", [128, 4 * KVC], F32, kind="ExternalInput").ap()
    d["tri"] = nc.dram_tensor("tri", [128, DGC * QBS], BF16, kind="ExternalInput").ap()
    d["ident"] = nc.dram_tensor("ident", [128, 128], BF16, kind="ExternalInput").ap()
    d["onescol"] = nc.dram_tensor("onescol", [128, 1], BF16, kind="ExternalInput").ap()
    d["onesrow"] = nc.dram_tensor("onesrow", [1, 128], BF16, kind="ExternalInput").ap()
    d["cvec"] = nc.dram_tensor("cvec", [128, 4], F32, kind="ExternalInput").ap()
    for w in ("Wq", "Wk", "Wv", "Wo"):
        d[w] = nc.dram_tensor(w, [E, E], F32, kind="ExternalInput").ap()
    d["wg"] = nc.dram_tensor("wg", [cfg.HID, E], I32, kind="ExternalInput").ap()
    d["wu"] = nc.dram_tensor("wu", [cfg.HID, E], I32, kind="ExternalInput").ap()
    d["wd"] = nc.dram_tensor("wd", [E, cfg.HID], I32, kind="ExternalInput").ap()
    d["out"] = nc.dram_tensor("out", [OWN, E], F32, kind="ExternalOutput").ap()
    gsp = nc.dram_tensor("g_spill", [OWN, E], F32, kind="Internal").ap()

    xq_t = d["x_q"].rearrange("(n p) e -> n p e", p=128)
    xkv_t = d["x_kv"].rearrange("(n p) e -> n p e", p=128)
    out_t = d["out"].rearrange("(n p) e -> n p e", p=128)
    gsp_t = gsp.rearrange("(n p) e -> n p e", p=128)
    wgr = d["wg"].rearrange("(h p) e -> h p e", p=128)
    wur = d["wu"].rearrange("(h p) e -> h p e", p=128)
    wdr = d["wd"].rearrange("(dc p) h -> p dc h", p=128)
    wqr = d["Wq"].rearrange("(ec p) e -> p ec e", p=128)
    wkr = d["Wk"].rearrange("(ec p) e -> p ec e", p=128)
    wvr = d["Wv"].rearrange("(ec p) e -> p ec e", p=128)
    wor = d["Wo"].rearrange("(rc p) e -> p rc e", p=128)

    P = PB(nc)

    with ExitStack() as top:
        P.mksem(top, "sT", "sS", "sV", "dC", "dX0", "dX1", "dW0", "dW1",
                "dS0", "dS1", "dV0", "dV1", "dT0", "dT1", "dTW0", "dTW1", "dG0", "dG1", "dR0", "dR1", "dR2", "dR3", "dO0", "dO1", "dP", "dK")
        e = top.enter_context

        sWL = e(nc.sbuf_tensor("sWL", [128, 4 * KVC], F32))
        sBM = [e(nc.sbuf_tensor(f"sBM{i}", [128, 4 * KVC], F32)) for i in range(2)]
        sID = e(nc.sbuf_tensor("sID", [128, 128], BF16))
        sONE = e(nc.sbuf_tensor("sONE", [128, 1], BF16))
        sOR = e(nc.sbuf_tensor("sOR", [1, 128], BF16))
        sCV = e(nc.sbuf_tensor("sCV", [128, 4], F32))
        RA = e(nc.sbuf_tensor("RA", [128, max(EC, NP) * OWN], BF16))
        big0c = max(NHC * OWN, EC * S + NP * OWN, EC * E)
        BIG0 = e(nc.sbuf_tensor("BIG0", [128, big0c], BF16))
        b5set = 2 * S + 2 * KVC * 130 + 4 * EC * 128 + 2 * 2 * QBS + 1024 + QBS
        big1c = max(NHC * E, b5set)
        BIG1 = e(nc.sbuf_tensor("BIG1", [128, big1c], BF16))
        sLN = [e(nc.sbuf_tensor(f"sLN{i}", [128, 8], F32)) for i in range(2)]
        sMQ = e(nc.sbuf_tensor("sMQ", [1, NP * 4], F32))
        sMK = e(nc.sbuf_tensor("sMK", [1, 8], F32))
        sBSB = e(nc.sbuf_tensor("sBSB", [1, 4], BF16))


        xn_own = RA[:, 0:EC * OWN].rearrange("p (ec t) -> p ec t", ec=EC)
        AO = RA[:, 0:NP * OWN].rearrange("p (np t) -> p np t", np=NP)
        ACT = BIG0[:, 0:NHC * OWN].rearrange("p (hc t) -> p hc t", hc=NHC)
        xn_kv = BIG0[:, 0:EC * S].rearrange("p (ec t) -> p ec t", ec=EC)
        Q_all = BIG0[:, EC * S:EC * S + NP * OWN].rearrange("p (np t) -> p np t", np=NP)
        Wo_sb = BIG0[:, 0:EC * E].rearrange("p (rc e) -> p rc e", rc=EC)
        wdT = BIG1[:, 0:NHC * E].rearrange("p (hc e) -> p hc e", hc=NHC)
        o = 0
        Kp = [BIG1[:, o + i * S: o + (i + 1) * S] for i in range(2)]
        o += 2 * S
        Vaug = [BIG1[:, o + i * KVC * 130: o + (i + 1) * KVC * 130]
                .rearrange("p (c two sv) -> p c two sv", two=2, sv=65) for i in range(2)]
        o += 2 * KVC * 130
        Wsl = [BIG1[:, o + i * EC * 128: o + (i + 1) * EC * 128]
               .rearrange("p (ec dd) -> p ec dd", ec=EC) for i in range(4)]
        o += 4 * EC * 128
        expb = [BIG1[:, o + i * 2 * QBS: o + (i + 1) * 2 * QBS] for i in range(2)]
        o += 2 * 2 * QBS
        sSQ = [BIG1[:, o:o + 512], BIG1[:, o + 512:o + 1024]]
        o += 1024
        sRS = BIG1[:, o:o + QBS]

        def ld(dst, src):
            return lambda sy, dst=dst, src=src: sy.dma_start(out=dst, in_=src)

        nconst = 0
        for dst, src in ((sWL[:], d["WL"]), (sID[:], d["ident"]),
                         (sONE[:], d["onescol"]),
                         (sOR[:], d["onesrow"]), (sCV[:], d["cvec"])):
            P.op("SY", ld(dst, src), inc="dC", amt=16)
            nconst += 1
        tok_consts = ("dC", 16 * nconst)

        def ln_phase(xt_src, ntile, xn_dst, dsem, gates, tag, pool_ext=None,
                     xnt_ext=None, xnt_gate=None,
                     dt_gate=None, sq_gate=None, interleave_cb=None,
                     sq_ext=None):
            with ExitStack() as ph:
                pe = ph.enter_context
                if pool_ext is not None:
                    xst, xnt = pool_ext
                else:
                    xst = [pe(nc.sbuf_tensor(f"xst{tag}{i}", [128, E], F32))
                           for i in range(2)]
                    xnt = xnt_ext if xnt_ext is not None else [
                        pe(nc.sbuf_tensor(f"xnt{tag}{i}", [128, E], BF16))
                        for i in range(2)]
                sq = sq_ext if sq_ext is not None else [
                    pe(nc.psum_tensor(f"sqb{tag}{i}", [128, E], F32))
                    for i in range(2)]
                xst_free = [None, None]
                sq_free = list(sq_gate) if sq_gate else [None, None]
                xnt_free = [xnt_gate, xnt_gate]
                chain = [None, None]
                dt_cur = [None, None]
                last_ev = None
                tile_toks = []
                for n in range(ntile):
                    par = n % 2
                    w = [xst_free[par]] + (gates if n < 2 else [])
                    tdma = P.op("SY", ld(xst[par][:], xt_src[n]),
                                waits=w, inc=dsem[par], amt=16)
                    L = sLN[par]
                    t1 = P.op("V", lambda v, L=L, b=xst[par][:]:
                              v.reduce_sum(L[:, 0:1], b, axis=AX.X),
                              waits=[tdma, chain[par]], inc="sV")
                    t2 = P.op("S", lambda s, a=sq[par][:], b=xst[par][:]:
                              s.activation(a, b, AF.Square),
                              waits=[tdma, sq_free[par]], inc="sS")
                    t3 = P.op("V", lambda v, L=L, a=sq[par][:]:
                              v.reduce_sum(L[:, 1:2], a, axis=AX.X),
                              waits=[t2], inc="sV")
                    sq_free[par] = t3
                    t4 = P.op("S", lambda s, L=L: s.activation(
                        L[:, 2:3], L[:, 0:1], AF.Copy, scale=1.0 / E),
                        waits=[t1], inc="sS")
                    t5 = P.op("S", lambda s, L=L: s.activation(
                        L[:, 3:4], L[:, 2:3], AF.Square),
                        waits=[t4], inc="sS")
                    t5b = P.op("S", lambda s, L=L: s.activation(
                        L[:, 5:6], L[:, 3:4], AF.Copy, scale=-1.0, bias=1e-5),
                        waits=[t5], inc="sS")
                    t6 = P.op("S", lambda s, L=L: s.activation(
                        L[:, 4:5], L[:, 1:2], AF.Copy, scale=1.0 / E),
                        waits=[t3], inc="sS")
                    t8 = P.op("S", lambda s, L=L: s.activation(
                        L[:, 6:7], L[:, 4:5], AF.Ln, bias=L[:, 5:6]),
                        waits=[t5b, t6], inc="sS")
                    t9 = P.op("S", lambda s, L=L: s.activation(
                        L[:, 7:8], L[:, 6:7], AF.Exp, scale=-0.5),
                        waits=[t8], inc="sS")
                    ta = P.op("S", lambda s, L=L: s.activation(
                        L[:, 3:4], L[:, 2:3], AF.Copy, scale=-1.0),
                        waits=[t9], inc="sS")
                    tb = P.op("S", lambda s, L=L: s.activation(
                        L[:, 5:6], L[:, 3:4], AF.Copy, scale=L[:, 7:8]),
                        waits=[ta], inc="sS")
                    t11 = P.op("V", lambda v, L=L, a=xnt[par][:], b=xst[par][:]:
                               v.tensor_scalar(a, b, L[:, 7:8], L[:, 5:6],
                                               ALU.mult, ALU.add),
                               waits=[tb, xnt_free[par]], inc="sV")
                    chain[par] = t11
                    xst_free[par] = t11
                    qn, dsT = ("S", "dT0") if par == 0 else ("SY", "dT1")
                    tdt = P.op(qn, lambda e,
                               o=xn_dst[:, :, n * 128:(n + 1) * 128],
                               i_=xnt[par][:]:
                               e.dma_start_transpose(out=o, in_=i_),
                               waits=[t11, dt_gate], inc=dsT, amt=16)
                    xnt_free[par] = tdt
                    dt_cur[par] = tdt
                    last_ev = [t for t in dt_cur if t is not None]
                    tile_toks.append(last_ev)
                    if interleave_cb is not None:
                        interleave_cb(n, tile_toks)
                return last_ev, tile_toks

        with ExitStack() as phb0:
            pb0 = phb0.enter_context
            xstA = [pb0(nc.sbuf_tensor(f"xstA{i}", [128, E], F32))
                    for i in range(2)]
            ph = ExitStack()
            pe = ph.enter_context
            wgc = [pe(nc.sbuf_tensor(f"wgc{i}", [128, E], BF16)) for i in range(2)]
            wuc = [pe(nc.sbuf_tensor(f"wuc{i}", [128, E], BF16)) for i in range(2)]
            wdc = [pe(nc.sbuf_tensor("wdc0", [128, E], BF16))]
            wgT = [pe(nc.sbuf_tensor(f"wgT{i}", [128, EC * 128], BF16))
                   for i in range(2)]
            wuT = [pe(nc.sbuf_tensor(f"wuT{i}", [128, EC * 128], BF16))
                   for i in range(2)]
            sSG = [pe(nc.sbuf_tensor(f"sSG{i}", [128, QW], BF16)) for i in range(2)]
            sGT = xstA

            xnt0 = [BIG0[:, big0c - (i + 1) * E: big0c - i * E] for i in range(2)]
            tok_xnown, xn_tiles = ln_phase(xq_t, OWNC, xn_own, ("dX0", "dX1"),
                                           [tok_consts], "a",
                                           pool_ext=(xstA, xnt0))
            tok_b0T = P.cnt("sT")
            tok_b0V = P.cnt("sV")
            tok_b0S = P.cnt("sS")
            P.flush("b0")

            act_done = [[None] * NQW for _ in range(NHC)]
            wdT_done = [None] * NHC

            with ExitStack() as php1:
                pp = php1.enter_context
                Gps = [pp(nc.psum_tensor(f"Gps{i}", [128, QW], F32)) for i in range(3)]
                Ups = [pp(nc.psum_tensor(f"Ups{i}", [128, QW], F32)) for i in range(3)]

                f_gate = []
                wgc_free = [None, None]
                wuc_free = [None, None]
                wdc_free = [None, None]
                wgT_free = [None, None]
                wuT_free = [None, None]
                gps_free = [None] * 3
                ups_free = [None] * 3
                sg_free = [None, None]
                qctr = [0]

                def transpose_into(src, dst3, waits, par, extra_last=None):
                    ww = list(waits)
                    if extra_last is not None:
                        ww.append(extra_last)
                    tdt = P.op("SY", lambda e, o=dst3, i_=src[:]:
                               e.dma_start_transpose(out=o, in_=i_),
                               waits=ww, inc="dTW0" if par == 0 else "dTW1",
                               amt=16)
                    return [tdt] * (EC // GT)

                for hc in range(NHC):
                    par = hc % 2
                    dsem = "dW0" if par == 0 else "dW1"
                    tg = P.op("G", lambda g, o=wgc[par][:], s=wgr[hc]:
                              g.dma_start(out=o, in_=s),
                              waits=[wgc_free[par]] + (f_gate if hc < 2 else []),
                              inc=dsem, amt=16)
                    tu = P.op("G", lambda g, o=wuc[par][:], s=wur[hc]:
                              g.dma_start(out=o, in_=s),
                              waits=[wuc_free[par]], inc=dsem, amt=16)
                    td = P.op("G", lambda g,
                              o=wdc[0][:].rearrange("p (dc h) -> p dc h", dc=EC),
                              s=wdr[:, :, hc * 128:(hc + 1) * 128]:
                              g.dma_start(out=o, in_=s),
                              waits=[wdc_free[0]], inc=dsem, amt=16)
                    transpose_into(
                        wgc[par],
                        wgT[par][:].rearrange("p (c t) -> p c t", c=EC),
                        [td], par, extra_last=wgT_free[par])
                    transpose_into(
                        wuc[par],
                        wuT[par][:].rearrange("p (c t) -> p c t", c=EC),
                        [td], par, extra_last=wuT_free[par])
                    d_toks = transpose_into(
                        wdc[0],
                        wdT[:, hc, :].rearrange("p (c t) -> p c t", c=EC),
                        [td], par)
                    tall = d_toks[-1]
                    g_toks = [tall] * (EC // GT)
                    u_toks = [tall] * (EC // GT)
                    wgc_free[par] = tall
                    wuc_free[par] = tall
                    wdc_free[0] = tall
                    wdT_done[hc] = tall
                    last_g = None
                    last_u = None
                    for qs in range(NQW):
                        gp = qctr[0] % 3
                        qctr[0] += 1
                        for ec in range(EC):
                            last_g = P.op(
                                "T", lambda t, o=Gps[gp][:],
                                l=wgT[par][:, ec * 128:(ec + 1) * 128],
                                r=xn_own[:, ec, qs * QW:(qs + 1) * QW],
                                st=(ec == 0), sp=(ec == EC - 1):
                                t.matmul(o, l, r, start=st, stop=sp),
                                waits=[g_toks[ec // GT], gps_free[gp]] +
                                      xn_tiles[((qs + 1) * QW) // 128 - 1],
                                inc="sT" if ec == EC - 1 else None)
                        for ec in range(EC):
                            last_u = P.op(
                                "T", lambda t, o=Ups[gp][:],
                                l=wuT[par][:, ec * 128:(ec + 1) * 128],
                                r=xn_own[:, ec, qs * QW:(qs + 1) * QW],
                                st=(ec == 0), sp=(ec == EC - 1):
                                t.matmul(o, l, r, start=st, stop=sp),
                                waits=[u_toks[ec // GT], ups_free[gp]],
                                inc="sT" if ec == EC - 1 else None)
                        tsg = P.op("S", lambda s, o=sSG[qs % 2][:], i_=Gps[gp][:]:
                                   s.activation(o, i_, AF.Silu, scale=1.0 / 127.0),
                                   waits=[last_g, sg_free[qs % 2]], inc="sS")
                        gps_free[gp] = tsg
                        sp_ = qs % 2
                        tm = P.op("V", lambda v, o=ACT[:, hc, qs * QW:(qs + 1) * QW],
                                  a=sSG[sp_][:], b=Ups[gp][:]:
                                  v.tensor_tensor(o, a, b, ALU.mult),
                                  waits=[tsg, last_u] +
                                        ([tok_b0T] if hc == NHC - 1 else []),
                                  inc="sV")
                        ups_free[gp] = tm
                        sg_free[qs % 2] = tm
                        act_done[hc][qs] = tm
                    wgT_free[par] = last_g
                    wuT_free[par] = last_u

                tok_f1T = P.cnt("sT")
                tok_f1V = P.cnt("sV")
                tok_f1S = P.cnt("sS")
                P.flush("f1")

            with ExitStack() as php2:
                pp = php2.enter_context
                Dps = [pp(nc.psum_tensor(f"Dps{i}", [128, E], F32)) for i in range(4)]
                dps_free = [None] * 4
                gt_free = [tok_b0V, tok_b0V]
                ng_store = 0
                for q2 in range(OWNC):
                    dp = q2 % 4
                    qs = (q2 * 128) // QW
                    last = None
                    for ew in range(NEW):
                        for hc in range(NHC):
                            w = [act_done[hc][qs], wdT_done[hc], dps_free[dp]]
                            if q2 == 0 and ew == 0 and hc == 0:
                                w += [tok_f1V, tok_f1S]
                            last = P.op(
                                "T", lambda t, o=Dps[dp][:, ew * EW:(ew + 1) * EW],
                                l=ACT[:, hc, q2 * 128:(q2 + 1) * 128],
                                r=wdT[:, hc, ew * EW:(ew + 1) * EW],
                                st=(hc == 0), sp=(hc == NHC - 1):
                                t.matmul(o, l, r, start=st, stop=sp),
                                waits=w, inc="sT" if hc == NHC - 1 else None)
                    gp = q2 % 2
                    tsil = P.op("S", lambda s, o=sGT[gp][:], i_=Dps[dp][:]:
                                s.activation(o, i_, AF.Silu, scale=C2),
                                waits=[last, gt_free[gp]], inc="sS")
                    dps_free[dp] = tsil
                    tst = P.op("G", lambda g, o=gsp_t[q2], i_=sGT[gp][:]:
                               g.dma_start(out=o, in_=i_),
                               waits=[tsil], inc="dG0" if gp == 0 else "dG1",
                               amt=16)
                    gt_free[gp] = tst
                    ng_store += 1
                tok_f2T = P.cnt("sT")
                tok_f2S = P.cnt("sS")
                P.flush("f2")
            ph.close()

            with ExitStack() as phc:
                pc = phc.enter_context
                sPK = pc(nc.sbuf_tensor("sPK", [128, S], BF16))
                sTRI = pc(nc.sbuf_tensor("sTRI", [128, DGC * QBS], BF16))
                sPQ = Kp[1][:, 2 * E:2 * E + OWN]
                gdone = [("dG0", 16 * ((ng_store + 1) // 2)),
                         ("dG1", 16 * (ng_store // 2))]
                cgate = [tok_f1T, tok_f1V, tok_f2T, tok_f2S] + gdone
                P.op("G", lambda g: g.dma_start(out=sPQ, in_=d["posq"]),
                     waits=[tok_f2T], inc="dP", amt=16)
                tpq = ("dP", 16)
                P.op("G", lambda g: g.dma_start(out=sPK[:], in_=d["posk"]),
                     waits=cgate, inc="dK", amt=16)
                tpk = ("dK", 16)


                tok_vaug = [None, None]
                for i in range(2):
                    tok_vaug[i] = P.op("V", lambda v, i=i: v.memset(
                        Vaug[i][:, :, :, 64:65], 1.0),
                        waits=[tok_f2T, tok_f2S], inc="sV")

                SC2W = max(2 * QBS, 2 * GT * 128, 2 * QW)
                SC2 = [pc(nc.psum_tensor(f"SC2_{i}", [128, SC2W], F32))
                       for i in range(2)]
                AVPD = max(QBS, 512)
                KVPD = max(KW, 512)
                AV2 = pc(nc.psum_tensor("AV2", [128, max(2 * AVPD, E)], F32))
                KV2 = pc(nc.psum_tensor("KV2", [128, max(2 * KVPD, E)], F32))
                AVps = [AV2[0:65, 0:QBS], AV2[0:65, AVPD:AVPD + QBS]]
                KVp = [KV2[:, 0:KW], KV2[:, KVPD:KVPD + KW]]
                Qps = [SC2[0][:, 0:QW], AV2[:, 0:QW], AV2[:, QW:2 * QW]]
                SQp1 = [SC2[1][0:1, 0:QW], SC2[1][0:1, QW:2 * QW]]
                wq_free = [None, None]
                qps_free = [None, None, None]
                sqst = dict(free=[tok_f2T, tok_f2T])
                b4st = dict(sqmm_free=[tok_f2S, tok_f2S], qctr=0)
                tok_mq = [None] * NP
                qbs_per_qs = max(1, 4 // NQW)

                qst = [dict(), dict()]

                def q_dma(p):
                    par = p % 2
                    dsem = "dS0" if par == 0 else "dS1"
                    qst[par]["tw"] = P.op(
                        "G", lambda g, o=Wsl[par][:],
                        s=wqr[:, :, p * 128:(p + 1) * 128]:
                        g.dma_start(out=o, in_=s),
                        waits=[wq_free[par], tok_f2T], inc=dsem, amt=16)

                def q_qs(p, qs):
                    par = p % 2
                    qd = qst[par]
                    qp = b4st["qctr"] % 3
                    b4st["qctr"] += 1
                    last = None
                    for ec in range(EC):
                        w = [qd["tw"], qps_free[qp]] + tok_xnown
                        if p == 0 and ec == 0:
                            w += [tok_f2S, tok_f1V]
                        last = P.op(
                            "T", lambda t, o=Qps[qp][:],
                            l=Wsl[par][:, ec, :],
                            r=xn_own[:, ec, qs * QW:(qs + 1) * QW],
                            st=(ec == 0), sp=(ec == EC - 1):
                            t.matmul(o, l, r, start=st, stop=sp),
                            waits=w, inc="sT" if ec == EC - 1 else None)
                    wq_free[par] = last
                    tev = P.op("V", lambda v,
                               o=Q_all[:, p, qs * QW:(qs + 1) * QW],
                               a=Qps[qp][:],
                               b=sPQ[:, qs * QW:(qs + 1) * QW]:
                               v.tensor_tensor(o, a, b, ALU.add),
                               waits=[last, tpq], inc="sV")
                    qps_free[qp] = tev
                    qd[("tev", qs)] = tev

                def q_sq(p, qs):
                    par = p % 2
                    qd = qst[par]
                    sqi = qs % 2
                    tsq = P.op("V", lambda v, o=sSQ[sqi][:, 0:QW],
                               a=Q_all[:, p, qs * QW:(qs + 1) * QW]:
                               v.tensor_tensor(o, a, a, ALU.mult),
                               waits=[qd[("tev", qs)], sqst["free"][sqi]],
                               inc="sV")
                    tmm = P.op("T", lambda t, o=SQp1[sqi][0:1, 0:QW],
                               l=sONE[:], r=sSQ[sqi][:, 0:QW]:
                               t.matmul(o, l, r, start=True, stop=True),
                               waits=[tsq, tok_consts, tok_f1V,
                                      b4st["sqmm_free"][sqi]], inc="sT")
                    sqst["free"][sqi] = tmm
                    rlast = None
                    for jj in range(qbs_per_qs):
                        j = qs * qbs_per_qs + jj
                        rlast = P.op(
                            "V", lambda v, p=p, j=j, jj=jj, sqi=sqi:
                            v.reduce_max(
                                sMQ[0:1, p * 4 + j:p * 4 + j + 1],
                                SQp1[sqi][0:1, jj * QBS:(jj + 1) * QBS],
                                axis=AX.X),
                            waits=[tmm, rlast], inc="sV")
                    b4st["sqmm_free"][sqi] = rlast
                    if qs == NQW - 1:
                        tok_mq[p] = rlast

                def qpair_units(p):
                    units = []

                    def u0(p=p):
                        q_dma(p)
                        q_qs(p, 0)
                    units.append((3 * p, u0))
                    for k in range(1, NQW):
                        def uk(p=p, k=k):
                            q_qs(p, k)
                            if k >= 2:
                                q_sq(p, k - 2)
                        units.append((3 * p + (k + 1) // 2, uk))

                    def ulast(p=p):
                        for k in range(max(0, NQW - 2), NQW):
                            q_sq(p, k)
                    units.append((3 * p + 2, ulast))
                    return units

                st = dict(kread=[tok_f2T, tok_f2T],
                          vread=[tok_vaug[0], tok_vaug[1]],
                          bm_free=[None, None],
                          sc2=[None, None],
                          kvb=[tok_f2S, tok_f2S],
                          av_free=[None, None],
                          expb_free=[tok_f2T] * 2,
                          sl_free=[None] * 4, bm_tok=[None, None],
                          k_tok=[None, None], v_tok=[None, None],
                          rcp_free=[None] * 4, rcf_free=[None, None],
                          rs_free=tok_f2T)
                kvst = [dict(), dict()]
                TPK = KW // 128

                def kv_units(p, ks, vs, xw):
                    par = p % 2
                    dsem = "dV0" if par == 0 else "dV1"
                    kv = kvst[par]
                    units = []

                    def u_dma():
                        kv["rm_tok2"] = None
                        kv["twk"] = P.op("G", lambda g, o=Wsl[ks][:],
                                         s=wkr[:, :, p * 128:(p + 1) * 128]:
                                         g.dma_start(out=o, in_=s),
                                         waits=[st["sl_free"][ks], tok_f2T],
                                         inc=dsem, amt=16)
                        kv["twv"] = P.op("G", lambda g, o=Wsl[vs][:],
                                         s=wvr[:, :, p * 128:(p + 1) * 128]:
                                         g.dma_start(out=o, in_=s),
                                         waits=[st["sl_free"][vs], tok_f2T],
                                         inc=dsem, amt=16)
                    units.append((0, u_dma))

                    def emit_kg(kg):
                        bi = kg % 2
                        bank = KVp[bi]
                        last = None
                        xt = xw((kg + 1) * TPK - 1)
                        for ec in range(EC):
                            w = [kv["twv"], st["kvb"][bi]] + xt
                            if ec == 0:
                                w.append(tok_f1V)
                            last = P.op(
                                "T", lambda t, o=bank[:, 0:KW],
                                l=Wsl[ks][:, ec, :],
                                r=xn_kv[:, ec, kg * KW:(kg + 1) * KW],
                                st_=(ec == 0), sp=(ec == EC - 1):
                                t.matmul(o, l, r, start=st_, stop=sp),
                                waits=w,
                                inc="sT" if ec == EC - 1 else None)
                        tev = P.op("V", lambda v,
                                   o=Kp[par][:, kg * KW:(kg + 1) * KW],
                                   a=bank[:, 0:KW],
                                   b=sPK[:, kg * KW:(kg + 1) * KW]:
                                   v.tensor_tensor(o, a, b, ALU.add),
                                   waits=[last, tpk, st["kread"][par]],
                                   inc="sV")
                        st["kvb"][bi] = tev
                        kv["lastk"] = last
                        kv["ktok"] = tev

                    for kg_ in range(NKG):
                        units.append(((kg_ + 1) * TPK - 1,
                                      lambda kg_=kg_: emit_kg(kg_)))

                    def u_setk():
                        st["sl_free"][ks] = kv["lastk"]
                        st["k_tok"][par] = kv["ktok"]
                    units.append((NKG * TPK - 1, u_setk))

                    def u_norm1(kg):
                        sqi = kg % 2
                        tsq = P.op("V", lambda v, sqi=sqi,
                                   o=sSQ[sqi][:, 0:KW],
                                   a=Kp[par][:, kg * KW:(kg + 1) * KW]:
                                   v.tensor_tensor(o, a, a, ALU.mult),
                                   waits=[st["kvb"][kg % 2],
                                          sqst["free"][sqi], kv["ktok"]],
                                   inc="sV")
                        bi = kg % 2
                        tmm = P.op("T", lambda t, bi=bi, sqi=sqi,
                                   o=KVp[bi][0:1, 0:KW]:
                                   t.matmul(o, sONE[:], sSQ[sqi][:, 0:KW],
                                            start=True, stop=True),
                                   waits=[tsq], inc="sT")
                        sqst["free"][sqi] = tmm
                        col = 1 if kg == 0 else 2
                        trd = P.op("V", lambda v, col=col, bi=bi:
                                   v.reduce_max(sMK[0:1, col:col + 1],
                                                KVp[bi][0:1, 0:KW],
                                                axis=AX.X),
                                   waits=[tmm, kv.get("rm_tok2")], inc="sV")
                        st["kvb"][bi] = trd
                        if kg == 0:
                            kv["rm_tok2"] = P.op(
                                "V", lambda v: v.tensor_copy(
                                    sMK[0:1, 0:1], sMK[0:1, 1:2]),
                                waits=[trd], inc="sV")
                        else:
                            kv["rm_tok2"] = P.op(
                                "V", lambda v: v.tensor_tensor(
                                    sMK[0:1, 0:1], sMK[0:1, 0:1],
                                    sMK[0:1, 2:3], ALU.max),
                                waits=[trd, kv["rm_tok2"]], inc="sV")
                        kv["rm_tok"] = kv["rm_tok2"]

                    for kg_ in range(NKG):
                        units.append(((kg_ + 1) * TPK - 1,
                                      lambda kg_=kg_: u_norm1(kg_)))

                    def emit_vc(c):
                        bi = c % 2
                        bank = KVp[bi]
                        last = None
                        xt = xw(c)
                        for ec in range(EC):
                            w = [kv["twv"], st["kvb"][bi]] + xt
                            if ec == 0:
                                w.append(tok_f1V)
                            last = P.op(
                                "T", lambda t, o=bank[:, 0:128],
                                l=xn_kv[:, ec, c * 128:(c + 1) * 128],
                                r=Wsl[vs][:, ec, :],
                                st_=(ec == 0), sp=(ec == EC - 1):
                                t.matmul(o, l, r, start=st_, stop=sp),
                                waits=w,
                                inc="sT" if ec == EC - 1 else None)
                        tev = P.op("S", lambda s_,
                                   o=Vaug[par][:, c, :, 0:64],
                                   i_=bank[:, 0:128].rearrange(
                                       "p (two sv) -> p two sv", two=2):
                                   s_.copy(o, i_),
                                   waits=[last, st["vread"][par]],
                                   inc="sS")
                        st["kvb"][bi] = tev
                        if c == KVC - 1:
                            st["sl_free"][vs] = last
                        kv["lastv"] = tev

                    for c_ in range(KVC):
                        units.append((c_, lambda c_=c_: emit_vc(c_)))

                    def u_bound():
                        st["v_tok"][par] = kv["lastv"]
                        t1 = P.op("V", lambda v, p=p: v.tensor_scalar(
                            sMK[0:1, 4:8], sMQ[0:1, p * 4:p * 4 + 4],
                            sMK[0:1, 0:1], 0.5, ALU.add, ALU.mult),
                            waits=[kv["rm_tok"], tok_mq[p]], inc="sV")
                        t2 = P.op("V", lambda v: v.tensor_scalar(
                            sMK[0:1, 4:8], sMK[0:1, 4:8], SCLAMP, -cfg.SM,
                            ALU.min, ALU.mult), waits=[t1], inc="sV")
                        t2b = P.op("V", lambda v: v.tensor_scalar(
                            sMK[0:1, 4:8], sMK[0:1, 4:8], 12.0, None,
                            ALU.add), waits=[t2], inc="sV")
                        t3 = P.op("V", lambda v: v.tensor_copy(
                            sBSB[:], sMK[0:1, 4:8]), waits=[t2b], inc="sV")
                        lastb = None
                        for j in range(4):
                            lastb = P.op(
                                "T", lambda t, j=j:
                                t.matmul(KVp[0][:, j:j + 1], sOR[0:1, :],
                                         sBSB[0:1, j:j + 1],
                                         start=True, stop=True),
                                waits=[t3, tok_consts, st["kvb"][0]],
                                inc="sT" if j == 3 else None)
                        bm_tok = None
                        for j in range(4):
                            bm_tok = P.op(
                                "V", lambda v, j=j, par=par:
                                v.tensor_scalar(
                                    sBM[par][:, j * KVC:(j + 1) * KVC],
                                    sWL[:, j * KVC:(j + 1) * KVC],
                                    KVp[0][:, j:j + 1], None, ALU.add),
                                waits=[lastb, tok_consts,
                                       st["bm_free"][par]],
                                inc="sV")
                        st["bm_tok"][par] = bm_tok
                        st["kvb"][0] = bm_tok
                    units.append((KVC - 1, u_bound))
                    return units

                ntile_kv = S // 128
                b34q = []
                for p_ in range(NP):
                    b34q += qpair_units(p_)
                holder = {}
                b34q += kv_units(0, 2, 3, lambda r: holder["toks"][r])
                b34q.sort(key=lambda it: it[0])
                qpos = [0]

                def b34_cb(n, toks):
                    holder["toks"] = toks
                    c = 0
                    while (qpos[0] < len(b34q) and b34q[qpos[0]][0] <= n
                           and c < 3):
                        b34q[qpos[0]][1]()
                        qpos[0] += 1
                        c += 1

                tok_xnkv, _ = ln_phase(
                    xkv_t, ntile_kv, xn_kv, ("dX0", "dX1"),
                    [tok_f1T, tok_f1V], "b",
                    xnt_ext=[Kp[1][:, i * E:(i + 1) * E]
                             for i in range(2)],
                    xnt_gate=tok_f2T,
                    dt_gate=tok_f2T, sq_gate=gdone,
                    sq_ext=[xstA[0], xstA[1]],
                    interleave_cb=b34_cb)
                while qpos[0] < len(b34q):
                    b34q[qpos[0]][1]()
                    qpos[0] += 1
                tok_b4T = P.cnt("sT")
                tok_b4V = P.cnt("sV")
                tok_b4S = P.cnt("sS")
                P.flush("b34")

                tok_Q = tok_b4V
                st["sc2"] = [tok_Q, tok_b4S]
                st["av_free"] = [tok_Q, tok_Q]
                st["sl_free"][0] = tok_b4T
                st["sl_free"][1] = tok_b4T

                with ExitStack() as ph5:
                    p5 = ph5.enter_context
                    sRCP = [p5(nc.sbuf_tensor(f"sRCP{i}", [1, QBS], BF16))
                            for i in range(4)]
                    sRCF = [p5(nc.sbuf_tensor(f"sRCF{i}", [1, QBS], F32))
                            for i in range(2)]
                    ttri = P.op("SY", ld(sTRI[:], d["tri"]),
                                waits=[tok_f1T, tok_f1V], inc="dC", amt=16)
                    fillers = []


                    def attention(p):
                        par = p % 2
                        for j in range(4):
                            ext = cfg.EXT[j]
                            avdep = [st["av_free"][0], st["av_free"][1]]
                            ev_tok = {}

                            def emit_sc(c):
                                bi = c % 2
                                lm = None
                                for h in range(2):
                                    lm = P.op(
                                        "T", lambda t, bi=bi, h=h, c=c, j=j, par=par,
                                        p=p:
                                        t.matmul(
                                            SC2[bi][:, h * QBS:(h + 1) * QBS],
                                            Kp[par][h * 64:(h + 1) * 64,
                                                    c * 128:(c + 1) * 128],
                                            Q_all[h * 64:(h + 1) * 64, p,
                                                  j * QBS:(j + 1) * QBS],
                                            start=True, stop=True),
                                        waits=[st["k_tok"][par], tok_Q,
                                               st["sc2"][bi]],
                                        inc="sT" if h == 1 else None)
                                st["kread"][par] = lm
                                col = j * KVC + c
                                te = P.op(
                                    "S", lambda s, bi=bi, col=col, par=par:
                                    s.activation(expb[bi],
                                                 SC2[bi][:, 0:2 * QBS], AF.Exp,
                                                 scale=cfg.SM,
                                                 bias=sBM[par][:, col:col + 1]),
                                    waits=[lm, st["bm_tok"][par],
                                           st["expb_free"][bi]],
                                    inc="sS")
                                st["sc2"][bi] = te
                                ev = te
                                if c >= ext - DGC:
                                    r = c - (ext - DGC)
                                    for h in range(2):
                                        ev = P.op(
                                            "V", lambda v, bi=bi, r=r, h=h:
                                            v.tensor_tensor(
                                                expb[bi][:, h * QBS:(h + 1) * QBS],
                                                expb[bi][:, h * QBS:(h + 1) * QBS],
                                                sTRI[:, r * QBS:(r + 1) * QBS],
                                                ALU.mult),
                                            waits=[te, ttri], inc="sV")
                                ev_tok[c] = ev

                            def emit_av(c):
                                bi = c % 2
                                lm = None
                                for h in range(2):
                                    w = [ev_tok.pop(c) if h == 1 else ev_tok[c],
                                         st["v_tok"][par]]
                                    if c == 0:
                                        w.append(avdep[h])
                                    lm = P.op(
                                        "T", lambda t, h=h, bi=bi, c=c, par=par,
                                        st_=(c == 0), sp=(c == ext - 1):
                                        t.matmul(AVps[h][:], Vaug[par][:, c, h, :],
                                                 expb[bi][:, h * QBS:(h + 1) * QBS],
                                                 start=st_, stop=sp),
                                        waits=w, inc="sT" if h == 1 else None)
                                if c == ext - 1:
                                    avdep[0] = lm
                                    avdep[1] = lm
                                st["vread"][par] = lm
                                st["expb_free"][bi] = lm

                            for c in range(ext):
                                if c >= 2:
                                    emit_av(c - 2)
                                emit_sc(c)
                                if c % 2 == 1 and fillers:
                                    fillers.pop(0)()
                            for c in range(max(0, ext - 2), ext):
                                emit_av(c)
                            if fillers:
                                fillers.pop(0)()
                            rp = 2 * (j % 2)
                            tcp = [None, None]
                            tbc = [None, None]
                            for h in range(2):
                                tl1 = P.op("S", lambda s, h=h:
                                           s.activation(sRCF[h][0:1, :],
                                                        AVps[h][64:65, :], AF.Ln),
                                           waits=[avdep[h], st["rcf_free"][h],
                                                  st["rcp_free"][rp + h],
                                                  tok_f1V, tok_f2S] + gdone,
                                           inc="sS")
                                tl2 = P.op("S", lambda s, h=h, rp=rp:
                                           s.activation(sRCP[rp + h][0:1, :],
                                                        sRCF[h][0:1, :],
                                                        AF.Exp, scale=-1.0),
                                           waits=[tl1], inc="sS")
                                st["rcf_free"][h] = tl2
                                tbc[h] = P.op(
                                    "T", lambda t, h=h, rp=rp:
                                    t.matmul(KVp[h][0:64, 0:QBS],
                                             sOR[0:1, 0:64], sRCP[rp + h][0:1, :],
                                             start=True, stop=True),
                                    waits=[tl2, tok_consts, st["kvb"][h]],
                                    inc="sT" if h == 1 else None)
                                st["rcp_free"][rp + h] = tbc[h]
                            for h in range(2):
                                tcp[h] = P.op(
                                    "V", lambda v, h=h:
                                    v.tensor_copy(sRS[h * 64:(h + 1) * 64, :],
                                                  KVp[h][0:64, 0:QBS]),
                                    waits=[tbc[1], st["rs_free"]], inc="sV")
                            st["kvb"][0] = tcp[1]
                            st["kvb"][1] = tcp[1]
                            for h in range(2):
                                tdv = P.op(
                                    "V", lambda v, h=h, p=p, j=j:
                                    v.tensor_tensor(
                                        AO[h * 64:(h + 1) * 64, p,
                                           j * QBS:(j + 1) * QBS],
                                        AVps[h][0:64, :],
                                        sRS[h * 64:(h + 1) * 64, :], ALU.mult),
                                    waits=[tcp[1], avdep[h], tok_b4T], inc="sV")
                                st["av_free"][h] = tdv
                                st["rs_free"] = tdv
                        st["bm_free"][par] = P.cnt("sS")
                        while fillers:
                            fillers.pop(0)()

                    for p in range(NP):
                        if p + 1 < NP:
                            fillers.extend(
                                fn for _, fn in kv_units(
                                    p + 1, (p + 1) % 2, 2 + (p + 1) % 2,
                                    lambda r: tok_xnkv))
                        attention(p)
                    two = P.op("G", lambda g: g.dma_start(
                        out=BIG0[:, 0:EC * E].rearrange("p (rc e) -> p rc e", rc=EC),
                        in_=wor),
                               waits=[st["k_tok"][(NP - 1) % 2],
                                      st["v_tok"][(NP - 1) % 2]],
                               inc="dS0", amt=16)
                    tok_b5T = P.cnt("sT")
                    tok_b5V = P.cnt("sV")
                    tok_b5S = P.cnt("sS")
                    P.flush("b5")

            if True:
                with ExitStack() as ph6:
                    p6 = ph6.enter_context
                    OPs = [p6(nc.psum_tensor(f"OPs{i}", [128, E], F32))
                           for i in range(4)]
                    gt6 = [p6(nc.sbuf_tensor(f"gt6{i}", [128, E], F32))
                           for i in range(2)]
                    xt6 = xstA
                    ot6 = [p6(nc.sbuf_tensor(f"ot6{i}", [128, E], F32))
                           for i in range(2)]

                    gates6 = [tok_b5T, tok_b5V, tok_b5S,
                              ("dG0", 16 * ((ng_store + 1) // 2)),
                              ("dG1", 16 * (ng_store // 2))]
                    ops_free = [None] * 4
                    g_free = [None, None]
                    x_free = [None, None]
                    o_free = [None, None]
                    t_free = None
                    nout = 0
                    for q2 in range(OWNC):
                        par = q2 % 2
                        dsem = "dR0" if par == 0 else "dR1"
                        tgl = P.op("G", lambda g, o=gt6[par][:], s=gsp_t[q2]:
                                   g.dma_start(out=o, in_=s),
                                   waits=[g_free[par]] + (gates6 if q2 < 2 else []),
                                   inc="dR2" if par == 0 else "dR3", amt=16)
                        txl = P.op("SY", lambda sy, o=xt6[par][:], s=xq_t[q2]:
                                   sy.dma_start(out=o, in_=s),
                                   waits=[x_free[par]] +
                                         (gates6 if q2 < 2 else []),
                                   inc=dsem, amt=16)
                        op = q2 % 4
                        last = None
                        for ew in range(NEW):
                            for rc in range(NP):
                                w = [ops_free[op]]
                                if q2 == 0 and ew == 0 and rc == 0:
                                    w += [tok_b5V, tok_b5S]
                                w += [tok_b5V, two]
                                last = P.op(
                                    "T", lambda t, op=op, ew=ew, rc=rc, q2=q2,
                                    st_=(rc == 0), sp=(rc == NP - 1):
                                    t.matmul(OPs[op][:, ew * EW:(ew + 1) * EW],
                                             AO[:, rc, q2 * 128:(q2 + 1) * 128],
                                             Wo_sb[:, rc, ew * EW:(ew + 1) * EW],
                                             start=st_, stop=sp),
                                    waits=w, inc="sT" if rc == NP - 1 else None)
                        lev = None
                        for ew in range(NEW):
                            tmul = P.op("V", lambda v, op=op, ew=ew, par=par:
                                        v.tensor_tensor(
                                            ot6[par][:, ew * EW:(ew + 1) * EW],
                                            OPs[op][:, ew * EW:(ew + 1) * EW],
                                            gt6[par][:, ew * EW:(ew + 1) * EW],
                                            ALU.mult),
                                        waits=[last, txl, tgl, t_free,
                                               o_free[par]],
                                        inc="sV")
                            lev = P.op("V", lambda v, op=op, ew=ew, par=par:
                                       v.tensor_tensor(
                                           ot6[par][:, ew * EW:(ew + 1) * EW],
                                           ot6[par][:, ew * EW:(ew + 1) * EW],
                                           xt6[par][:, ew * EW:(ew + 1) * EW],
                                           ALU.add),
                                       waits=[tmul], inc="sV")
                            t_free = lev
                        ops_free[op] = lev
                        g_free[par] = lev
                        x_free[par] = lev
                        tst = P.op("SY", lambda sy, o=out_t[q2], i_=ot6[par][:]:
                                   sy.dma_start(out=o, in_=i_),
                                   waits=[lev], inc="dO0" if par == 0 else "dO1",
                                   amt=16)
                        o_free[par] = tst
                        nout += 1
                    P.op("SY", lambda sy: sy.nop(),
                         waits=[("dO0", 16 * ((nout + 1) // 2)),
                                ("dO1", 16 * (nout // 2))])
                    P.flush("b6")

    return nc



def kernel(L0_attn_x, L0_attn_Wq, L0_attn_Wk, L0_attn_Wv, L0_attn_Wo,
           L0_ffn_w_gate, L0_ffn_w_up, L0_ffn_w_down):
    cfg = FULL
    x = np.asarray(L0_attn_x, dtype=np.float32)
    maps, owns = host_inputs(
        cfg, x,
        np.asarray(L0_attn_Wq, np.float32), np.asarray(L0_attn_Wk, np.float32),
        np.asarray(L0_attn_Wv, np.float32), np.asarray(L0_attn_Wo, np.float32),
        np.asarray(L0_ffn_w_gate, np.int32), np.asarray(L0_ffn_w_up, np.int32),
        np.asarray(L0_ffn_w_down, np.int32))
    nc = build_program(cfg)
    res = run_bass_kernel_spmd(nc, maps, core_ids=list(range(cfg.ncores)))
    out = np.empty((cfg.NB, cfg.S, cfg.E), np.float32)
    for c, (b, own) in enumerate(owns):
        out[b][own] = np.asarray(res.results[c]["out"], np.float32)
    return out
